# version 14
# baseline (speedup 1.0000x reference)
import numpy as np
import ml_dtypes
import concourse.bass as bass
import concourse.mybir as mybir
from concourse.bass_utils import run_bass_kernel_spmd
from concourse.alu_op_type import AluOpType as ALU

F32 = mybir.dt.float32
F32R = mybir.dt.float32r
BF16 = mybir.dt.bfloat16
AF = mybir.ActivationFunctionType
AX = mybir.AxisListType
BF = ml_dtypes.bfloat16

B_, S_, D = 4, 2048, 2048
T = 2048
TH = 512
NTH = T // TH
TOFF = 1024
NCH = D // 128
G = 8
DH = 1024
NH, HD, RD, LAT = 16, 128, 64, 512
LCH = LAT // 128
I_ = 1024
ICH = I_ // 128
NR = 7
QSCALE = float(1.0 / np.sqrt(HD))
TWO_PI = float(2.0 * np.pi)
INV_2PI = float(1.0 / (2.0 * np.pi))
RND_C = float(3 * 2 ** 22)
HALF_PI = float(np.pi / 2)
PI_LO = float(np.nextafter(np.float32(np.pi), np.float32(0.0)))
DEBUG = False
PHASE = "ALL"


class Sem:
    def __init__(self, nc, name):
        self.h = nc.alloc_semaphore(name)
        self.n = 0


class Builder:
    ENGINES = ("sync", "vector", "scalar", "tensor", "gpsimd")

    def __init__(self, nc):
        self.nc = nc
        self.prog = {e: [] for e in self.ENGINES}
        self.eng_sem = {}

    def sem(self, name):
        return Sem(self.nc, name)

    def emit(self, eng, fn, inc=None, amt=1):
        auto = self.eng_sem.get(eng)
        if auto is not None:
            assert inc is None or inc is auto, f"{eng} op must inc its own sem"
            self.wait(eng, auto)
            inc = auto
        if inc is None:
            self.prog[eng].append(fn)
            return None
        inc.n += amt
        self.prog[eng].append(
            lambda e, fn=fn, s=inc.h, a=amt: fn(e).then_inc(s, a))
        return inc.n

    def dma(self, eng, out, in_, sem):
        sem.n += 16
        self.prog[eng].append(
            lambda e, o=out, i=in_, s=sem.h: e.dma_start(out=o, in_=i)
            .then_inc(s, 16))
        return sem.n

    def wait(self, eng, sem, val=None):
        v = sem.n if val is None else val
        assert v <= sem.n, "wait on future event"
        self.prog[eng].append(lambda e, s=sem.h, v=v: e.wait_ge(s, v))

    def build(self):
        nc = self.nc
        with nc.Block() as block:
            @block.sync
            def _(e):
                for f in self.prog["sync"]:
                    f(e)

            @block.vector
            def _(e):
                for f in self.prog["vector"]:
                    f(e)

            @block.scalar
            def _(e):
                for f in self.prog["scalar"]:
                    f(e)

            @block.tensor
            def _(e):
                for f in self.prog["tensor"]:
                    f(e)

            @block.gpsimd
            def _(e):
                for f in self.prog["gpsimd"]:
                    f(e)
        return nc


def build_program(phase=PHASE, debug=DEBUG):
    nc = bass.Bass("TRN2", target_bir_lowering=False)
    B = Builder(nc)

    def din(name, shape, dtype=F32R):
        return nc.dram_tensor(name, list(shape), dtype,
                              kind="ExternalInput").ap()

    xT = din("xT", [D, T])
    W_in = din("W_in", [D, D], F32)
    Wg_x = din("Wg_x", [G, 256, 128], F32)
    Wg_a = din("Wg_a", [G, 256, 128], F32)
    W_out = din("W_out", [D, D], F32)
    b_in = din("b_in", [D], F32)
    bg_x = din("bg_x", [DH], F32)
    bg_a = din("bg_a", [DH], F32)
    b_out = din("b_out", [D], F32)
    a_real = din("a_real", [DH], F32)
    a_imag = din("a_imag", [DH], F32)
    kv_d_w = din("kv_d_w", [D, LAT], F32)
    q_d_w = din("q_d_w", [D, LAT], F32)
    k_u_w = din("k_u_w", [LAT, DH], F32)
    v_u_w = din("v_u_w", [LAT, D], F32)
    q_u_w = din("q_u_w", [LAT, DH], F32)
    rope_k_w = din("rope_k_w", [D, DH], F32)
    rope_q_w = din("rope_q_w", [LAT, DH], F32)
    o_w = din("o_w", [D, D], F32)
    ln1_w = din("ln1_w", [D], F32)
    ln2_w = din("ln2_w", [D], F32)
    sh_gate = din("sh_gate", [D, I_], F32)
    sh_up = din("sh_up", [D, I_], F32)
    sh_down = din("sh_down", [I_, D], F32)
    r_gate = din("r_gate", [NR, D, I_], F32)
    r_up = din("r_up", [NR, D, I_], F32)
    r_down = din("r_down", [NR, I_, D], F32)
    router_w = din("router_w", [D, NR], F32)
    router_b = din("router_b", [NR], F32)
    cosk_t = din("cosk_t", [128, T], F32)
    sinkS_t = din("sinkS_t", [128, T], F32)
    cosq_t = din("cosq_t", [128, T], F32)
    sinqS_t = din("sinqS_t", [128, T], F32)
    mask_t = din("mask_t", [T, TOFF], BF16)
    gate_row = din("gate_row", [1, T], F32)
    ktb_col = din("ktb_col", [128, 1], F32)
    ones_colr = din("ones_colr", [128, 1])
    ones_row = din("ones_row", [1, 128], F32)
    one_1 = din("one_1", [1, 1], F32)
    ones_bf = din("ones_bf", [128, 128], BF16)

    out_d = nc.dram_tensor("out", [D, TOFF], F32, kind="ExternalOutput").ap()
    dbg = {}
    if debug:
        dbg["x1"] = nc.dram_tensor("d_x1", [D, T], F32R,
                                   kind="ExternalOutput").ap()
        dbg["h"] = nc.dram_tensor("d_h", [D, TH], F32R,
                                  kind="ExternalOutput").ap()
        dbg["h1"] = nc.dram_tensor("d_h1", [D, TOFF], F32,
                                   kind="ExternalOutput").ap()
        dbg["w"] = nc.dram_tensor("d_w", [128, 8 * NR], F32,
                                  kind="ExternalOutput").ap()

    g_bf = nc.dram_tensor("g_bf", [8, D, I_], BF16).ap()
    u_bf = nc.dram_tensor("u_bf", [8, D, I_], BF16).ap()
    d_bf = nc.dram_tensor("d_bf", [8, I_, D], BF16).ap()
    wgx_bf = nc.dram_tensor("wgx_bf", [G, 256, 128], BF16).ap()
    wga_bf = nc.dram_tensor("wga_bf", [G, 256, 128], BF16).ap()
    win_bf = nc.dram_tensor("win_bf", [D, D], BF16).ap()
    wout_bf = nc.dram_tensor("wout_bf", [D, D], BF16).ap()
    kvd_bf = nc.dram_tensor("kvd_bf", [D, LAT], BF16).ap()
    qd_bf = nc.dram_tensor("qd_bf", [D, LAT], BF16).ap()
    ku_bf = nc.dram_tensor("ku_bf", [LAT, DH], BF16).ap()
    vu_bf = nc.dram_tensor("vu_bf", [LAT, D], BF16).ap()
    qu_bf = nc.dram_tensor("qu_bf", [LAT, DH], BF16).ap()
    ropek_bf = nc.dram_tensor("ropek_bf", [D, DH], BF16).ap()
    ropeq_bf = nc.dram_tensor("ropeq_bf", [LAT, DH], BF16).ap()
    ow_bf = nc.dram_tensor("ow_bf", [D, D], BF16).ap()
    x1_d = nc.dram_tensor("x1_d", [D, T], F32R).ap()
    xo_d = nc.dram_tensor("xo_d", [D, TOFF], F32).ap()
    h1_d = nc.dram_tensor("h1_d", [D, TOFF], F32).ap()
    k_d = nc.dram_tensor("k_d", [NH * 128, T], BF16).ap()
    v_d = nc.dram_tensor("v_d", [NH * T, HD], BF16).ap()
    q_dram = nc.dram_tensor("q_dram", [NH * 128, TOFF], BF16).ap()

    sb = nc.alloc_sbuf_tensor
    SA = sb("SA", [128, NCH * TH], F32R).ap()
    SB = sb("SB", [128, NCH * TH], F32).ap()
    SD = sb("SD", [128, NCH * TH], BF16).ap()
    SE = sb("SE", [128, NCH * TH], BF16).ap()
    SF = sb("SF", [128, NCH * 1024], BF16).ap()
    NSL = 10
    SPV = sb("SPV", [128, NSL * TH], F32).ap()
    SQT = sb("SQT", [128, TH], F32R).ap()
    WBUFB = sb("WBUFB", [128, 4 * 512], BF16).ap()
    QKT = SE
    SBR = SB.bitcast(F32R)
    CONST = sb("CONST", [128, 152], F32).ap()
    RW = sb("RW", [128, NCH * NR], F32).ap()
    WL2 = sb("WL2", [128, 2 * NR], F32).ap()
    RB = sb("RB", [128, NR], F32).ap()
    WTOK = sb("WTOK", [128, 8 * NR], F32).ap()
    CARRY = sb("CARRY", [128, 3 * G], F32).ap()
    GTT = sb("GTT", [128, TH], F32).ap()
    GT = GTT
    MROW = sb("MROW", [1, 3 * NH], F32).ap()
    ONECR = sb("ONECR", [128, 1], F32R).ap()
    ONER = sb("ONER", [1, 128], F32).ap()
    ONE1 = sb("ONE1", [1, 1], F32).ap()
    ONEBF = sb("ONEBF", [128, 128], BF16).ap()

    ps = nc.alloc_psum_tensor
    PA = ps("PA", [128, 1024], F32).ap()
    PB = ps("PB", [128, 1024], F32).ap()
    PC = ps("PC", [128, 1024], F32).ap()
    PD = ps("PD", [128, 1024], F32).ap()

    C_BIN, C_BOUT = 0, 16
    C_BGX, C_BGA, C_SP8, C_AI = 32, 40, 48, 56
    C_LN1, C_LN2 = 64, 80
    C_MH = 96
    C_T0, C_T1 = 112, 113
    C_EPS, C_ONE = 114, 115
    C_MHX = 116
    C_KTB = 148

    def ccol(i):
        return CONST[:, i:i + 1]

    s_lw = B.sem("s_lw")
    s_ls = B.sem("s_ls")
    s_lg = B.sem("s_lg")
    s_x = B.sem("s_x")
    s_gr = B.sem("s_gr")
    s_st = B.sem("s_st")
    s_sl = [B.sem("s_sl%d" % i) for i in range(4)]
    s_kv2 = [B.sem("s_kv0"), B.sem("s_kv1")]
    s_qt2 = [B.sem("s_qt0"), B.sem("s_qt1")]
    s_cast = B.sem("s_cast")
    s_t = B.sem("s_t")
    s_v = B.sem("s_v")
    s_s = B.sem("s_s")
    B.eng_sem = {"vector": s_v, "scalar": s_s}

    def barrier(engines=Builder.ENGINES):
        for eng in engines:
            for s in (s_lw, s_ls, s_lg, s_x, s_gr, s_st, s_t, s_v, s_s,
                      s_sl[0], s_sl[1], s_sl[2], s_sl[3],
                      s_kv2[0], s_kv2[1], s_qt2[0], s_qt2[1]):
                B.wait(eng, s)

    CSEM = {k: B.sem("s_cast_" + k) for k in
            ("win", "wg", "wout", "proj", "ow", "moe")}
    CAST = {}
    B.dma("gpsimd", win_bf, W_in, CSEM["win"])
    CAST["win"] = ("win", CSEM["win"].n)
    for g in range(G):
        B.dma("gpsimd", wgx_bf[g], Wg_x[g], CSEM["wg"])
        B.dma("gpsimd", wga_bf[g], Wg_a[g], CSEM["wg"])
    CAST["wg"] = ("wg", CSEM["wg"].n)
    B.dma("gpsimd", wout_bf, W_out, CSEM["wout"])
    CAST["wout"] = ("wout", CSEM["wout"].n)
    B.dma("gpsimd", kvd_bf, kv_d_w, CSEM["proj"])
    B.dma("gpsimd", qd_bf, q_d_w, CSEM["proj"])
    B.dma("gpsimd", ku_bf, k_u_w, CSEM["proj"])
    B.dma("gpsimd", vu_bf, v_u_w, CSEM["proj"])
    B.dma("gpsimd", qu_bf, q_u_w, CSEM["proj"])
    B.dma("gpsimd", ropek_bf, rope_k_w, CSEM["proj"])
    B.dma("gpsimd", ropeq_bf, rope_q_w, CSEM["proj"])
    CAST["proj"] = ("proj", CSEM["proj"].n)

    for k in range(NCH):
        B.dma("sync", ccol(C_BIN + k), b_in[k * 128:(k + 1) * 128].unsqueeze(1),
              s_lw)
        B.dma("sync", ccol(C_BOUT + k),
              b_out[k * 128:(k + 1) * 128].unsqueeze(1), s_lw)
        B.dma("sync", ccol(C_LN1 + k),
              ln1_w[k * 128:(k + 1) * 128].unsqueeze(1), s_lw)
        B.dma("sync", ccol(C_LN2 + k),
              ln2_w[k * 128:(k + 1) * 128].unsqueeze(1), s_lw)
        B.dma("sync", RW[:, k * NR:(k + 1) * NR],
              router_w[k * 128:(k + 1) * 128, :], s_lw)
    for g in range(G):
        B.dma("sync", ccol(C_BGX + g), bg_x[g * 128:(g + 1) * 128].unsqueeze(1),
              s_lw)
        B.dma("sync", ccol(C_BGA + g), bg_a[g * 128:(g + 1) * 128].unsqueeze(1),
              s_lw)
        B.dma("sync", ccol(C_SP8 + g),
              a_real[g * 128:(g + 1) * 128].unsqueeze(1), s_lw)
        B.dma("sync", ccol(C_AI + g),
              a_imag[g * 128:(g + 1) * 128].unsqueeze(1), s_lw)
    B.dma("sync", ONECR, ones_colr, s_lw)
    B.dma("sync", ONER, ones_row, s_lw)
    B.dma("sync", ONE1, one_1, s_lw)
    B.dma("sync", ONEBF, ones_bf, s_lw)
    B.dma("sync", ccol(C_KTB), ktb_col, s_lw)
    B.dma("sync", RB, router_b.unsqueeze(0).broadcast_to([128, NR]), s_lw)
    barrier()

    B.emit("vector", lambda e: e.memset(MROW, 0.0), inc=s_v)
    B.emit("vector", lambda e: e.memset(ccol(C_EPS), 1e-5))
    B.emit("vector", lambda e: e.memset(ccol(C_ONE), 1.0), inc=s_v)
    B.wait("scalar", s_v)

    t0, t1 = ccol(C_T0), ccol(C_T1)
    for g in range(G):
        a_g = ccol(C_SP8 + g)
        B.emit("scalar", lambda e, a=a_g: e.activation(t0, a, AF.Abs))
        B.emit("scalar", lambda e: e.activation(t0, t0, AF.Exp, scale=-1.0))
        B.emit("scalar", lambda e: e.activation(t0, t0, AF.Ln, bias=ccol(C_ONE)),
               inc=s_s)
        B.wait("vector", s_s)
        B.emit("vector", lambda e, a=a_g: e.tensor_scalar_max(t1, a, 0.0),
               inc=s_v)
        B.wait("vector", s_v)
        B.emit("vector", lambda e: e.tensor_add(t1, t1, t0), inc=s_v)
        B.wait("vector", s_v)
        B.emit("vector", lambda e, a=a_g: e.tensor_scalar_mul(a, t1, -8.0),
               inc=s_v)
        B.wait("scalar", s_v)
    guard = {0: None, 1: None}
    bslab_guard = [None, None, None, None]

    def gemm(W, KCH, MOUT, rhs_fn, evac, rhs_ready=None, n_free=TH,
             cast_ready=None):
        MB = 512
        nmb = (MOUT + MB - 1) // MB
        if rhs_ready is not None:
            B.wait("tensor", rhs_ready[0], rhs_ready[1])
        if cast_ready is not None:
            B.wait("sync", CSEM[cast_ready[0]], cast_ready[1])
        for mb in range(nmb):
            m0 = mb * MB
            mbw = min(MB, MOUT - m0)
            nmc = (mbw + 127) // 128
            pbank = (PA, PB) if mb % 2 == 0 else (PC, PD)
            if guard[mb % 2] is not None:
                B.wait("tensor", *guard[mb % 2])
                guard[mb % 2] = None
            for k in range(KCH):
                si = (mb * KCH + k) % 4
                sl_ = WBUFB[:, si * 512:si * 512 + mbw]
                if bslab_guard[si] is not None:
                    B.wait("sync", *bslab_guard[si])
                wv = B.dma("sync", sl_, W[k * 128:(k + 1) * 128, m0:m0 + mbw],
                           s_sl[si])
                B.wait("tensor", s_sl[si], wv)
                tv = None
                for mc in range(nmc):
                    mw = min(128, mbw - mc * 128)
                    pt = pbank[mc // 2][:, (mc % 2) * 512:(mc % 2) * 512 + n_free]
                    tv = B.emit(
                        "tensor",
                        lambda e, p=pt[0:mw, :], w=sl_[:, mc * 128:mc * 128 + mw],
                        r=rhs_fn(k), st=(k == 0), sp=(k == KCH - 1):
                        e.matmul(p, w, r, start=st, stop=sp),
                        inc=(s_t if mc == nmc - 1 else None))
                bslab_guard[si] = (s_t, tv)
            mm_done = s_t.n
            last = None
            for mc in range(nmc):
                mw = min(128, mbw - mc * 128)
                pt = pbank[mc // 2][:, (mc % 2) * 512:(mc % 2) * 512 + n_free]
                last = evac(m0 + mc * 128, mw, pt[0:mw, :], mm_done)
            guard[mb % 2] = last

    def norm(src_fn, kind, out_fn=None, extra=None):
        if guard[1] is not None:
            B.wait("tensor", *guard[1])
            guard[1] = None
        sum_r = PC[0:1, 0:TH]
        sq_r = PC[0:1, 512:512 + TH]
        sqv = {}
        for k in range(NCH):
            xt = src_fn(k)
            sqt = SQT[:, 0:TH]
            if k >= 1:
                B.wait("vector", s_t, sqv[k - 1])
            B.emit("vector", lambda e, x=xt, q=sqt: e.tensor_mul(q, x, x),
                   inc=s_v)
            B.wait("tensor", s_v)
            B.emit("tensor", lambda e, x=xt, st=(k == 0), sp=(k == NCH - 1):
                   e.matmul(sum_r, ONECR, x, start=st, stop=sp))
            sqv[k] = B.emit("tensor",
                            lambda e, q=sqt, st=(k == 0), sp=(k == NCH - 1):
                            e.matmul(sq_r, ONECR, q, start=st, stop=sp),
                            inc=s_t)
        B.wait("vector", s_t)
        mean = SPV[0:1, 0:TH]
        var = SPV[0:1, TH:2 * TH]
        rstd = SPV[0:1, 2 * TH:3 * TH]
        nmr = SPV[0:1, 3 * TH:4 * TH]
        B.emit("vector", lambda e: e.tensor_scalar_mul(mean, sum_r, 1.0 / D))
        B.emit("vector", lambda e: e.tensor_scalar_mul(var, sq_r, 1.0 / D),
               inc=s_v)
        B.wait("vector", s_v)
        B.emit("vector", lambda e: e.tensor_mul(nmr, mean, mean), inc=s_v)
        B.wait("vector", s_v)
        B.emit("vector", lambda e: e.tensor_sub(var, var, nmr), inc=s_v)
        B.wait("scalar", s_v)
        B.emit("scalar", lambda e: e.activation(rstd, var, AF.Ln, bias=ccol(C_EPS)[0:1, :]))
        B.emit("scalar", lambda e: e.activation(rstd, rstd, AF.Exp,
                                                scale=-0.5), inc=s_s)
        B.wait("vector", s_s)
        rows = dict(mean=mean, var=var, rstd=rstd)
        if extra is not None:
            extra(rows)
        B.emit("vector", lambda e: e.tensor_mul(nmr, mean, rstd), inc=s_v)
        B.wait("vector", s_v)
        B.emit("vector", lambda e: e.tensor_scalar_mul(nmr, nmr, -1.0),
               inc=s_v)
        B.wait("tensor", s_v)
        B.emit("tensor", lambda e: e.matmul(PD[:, 0:TH], ONER, rstd,
                                            start=True, stop=True))
        B.emit("tensor", lambda e: e.matmul(PD[:, 512:512 + TH], ONER, nmr,
                                            start=True, stop=True), inc=s_t)
        B.wait("vector", s_t)
        if out_fn is not None:
            for k in range(NCH):
                xt = src_fn(k)
                ot = out_fn(k)
                B.emit("vector", lambda e, x=xt: e.tensor_mul(
                    SPV[:, 4 * TH:5 * TH], x, PD[:, 0:TH]), inc=s_v)
                B.wait("vector", s_v)
                B.emit("vector", lambda e, o=ot: e.tensor_add(
                    o, SPV[:, 4 * TH:5 * TH], PD[:, 512:512 + TH]), inc=s_v)
        B.emit("vector", lambda e: e.nop(), inc=s_v)
        guard[1] = (s_v, s_v.n)
        return s_v.n

    def stage_b(th):
        tok = slice(th * TH, (th + 1) * TH)
        B.wait("sync", s_v)
        B.wait("sync", s_st)
        for k in range(NCH):
            B.dma("sync", SA[:, k * TH:(k + 1) * TH],
                  xT[k * 128:(k + 1) * 128, tok], s_x)
        XLD = s_x.n
        B.wait("vector", s_x, XLD)
        B.wait("tensor", s_x, XLD)
        ln_ms = norm(lambda k: SA[:, k * TH:(k + 1) * TH], "ln",
                     out_fn=lambda k: SF[:, k * TH:(k + 1) * TH])

        def evac_xh(m0, mw, p, tv):
            k = m0 // 128
            B.wait("vector", s_t, tv)
            v = B.emit("vector", lambda e, p=p, k=k: e.tensor_scalar(
                SB[:, k * TH:(k + 1) * TH], p, ccol(C_BIN + k), None,
                ALU.add), inc=s_v)
            return (s_v, v)

        gemm(win_bf, NCH, D, lambda k: SF[:, k * TH:(k + 1) * TH], evac_xh,
             rhs_ready=(s_v, ln_ms), cast_ready=CAST["win"])
        XH_OK = s_v.n
        B.wait("vector", s_v, XH_OK)
        B.wait("vector", s_lg)
        B.wait("vector", s_ls)
        for k in range(NCH):
            B.emit("vector", lambda e, k=k: e.tensor_copy(
                SE[:, k * 512:(k + 1) * 512], SB[:, k * TH:(k + 1) * TH]))
        B.emit("vector", lambda e: e.nop(), inc=s_v)
        XHB_OK = s_v.n

        if guard[1] is not None:
            B.wait("tensor", *guard[1])
            guard[1] = None
        B.wait("sync", s_v)
        B.wait("sync", s_t)
        gv_ = B.dma("sync", SPV[0:1, 0:TH], gate_row[0:1, tok], s_gr)
        B.wait("tensor", s_gr, gv_)
        B.emit("tensor", lambda e: e.matmul(
            PD[:, 0:TH], ONER, SPV[0:1, 0:TH], start=True, stop=True),
            inc=s_t)
        B.wait("vector", s_t)
        B.emit("vector", lambda e: e.tensor_copy(GT[:, 0:TH], PD[:, 0:TH]),
               inc=s_v)
        B.emit("vector", lambda e: e.nop(), inc=s_v)
        guard[1] = (s_v, s_v.n)
        def sl(i):
            return SPV[:, i * TH:(i + 1) * TH]

        for g in range(G):
            B.wait("tensor", s_v, XHB_OK)
            for wi, (wsrc, bc, dst) in enumerate(
                    ((wgx_bf, C_BGX + g, sl(1)), (wga_bf, C_BGA + g, sl(0)))):
                si = wi % 2
                if bslab_guard[si] is not None:
                    B.wait("sync", *bslab_guard[si])
                B.wait("sync", CSEM[CAST["wg"][0]], CAST["wg"][1])
                B.dma("sync", WBUFB[:, si * 512:si * 512 + 128],
                      wsrc[g, 0:128, :], s_sl[si])
                wv = B.dma("sync", WBUFB[:, si * 512 + 128:si * 512 + 256],
                           wsrc[g, 128:256, :], s_sl[si])
                B.wait("tensor", s_sl[si], wv)
                if guard[0] is not None:
                    B.wait("tensor", *guard[0])
                    guard[0] = None
                tv = None
                for kk in range(2):
                    tv = B.emit("tensor", lambda e, kk=kk, si=si, g=g:
                                e.matmul(PA[:, 0:TH],
                                         WBUFB[:, si * 512 + kk * 128:
                                               si * 512 + (kk + 1) * 128],
                                         SE[:, (2 * g + kk) * 512:
                                            (2 * g + kk + 1) * 512],
                                         start=(kk == 0), stop=(kk == 1)),
                                inc=(s_t if kk == 1 else None))
                bslab_guard[si] = (s_t, tv)
                B.wait("scalar", s_t, tv)
                sv = B.emit("scalar", lambda e, d=dst, bc=bc: e.activation(
                    d, PA[:, 0:TH], AF.Sigmoid, bias=ccol(bc)), inc=s_s)
                guard[0] = (s_s, sv)
            B.wait("vector", s_s)
            B.emit("vector", lambda e, g=g: e.tensor_scalar(
                sl(5), sl(0), ccol(C_SP8 + g), None, ALU.mult))
            B.emit("vector", lambda e, g=g: e.tensor_scalar(
                sl(2), sl(0), ccol(C_AI + g), None, ALU.mult), inc=s_v)
            B.wait("scalar", s_v)
            B.emit("scalar", lambda e: e.activation(sl(3), sl(5), AF.Exp),
                   inc=s_s)
            B.wait("vector", s_s)
            B.emit("vector", lambda e: e.tensor_mul(sl(8), sl(3), sl(3)),
                   inc=s_v)
            B.wait("vector", s_v)
            B.emit("vector", lambda e: e.tensor_scalar(
                sl(8), sl(8), -1.0, 1.0, ALU.mult, ALU.add), inc=s_v)
            B.wait("scalar", s_v)
            B.emit("scalar", lambda e: e.activation(sl(8), sl(8), AF.Sqrt),
                   inc=s_s)
            B.wait("vector", s_s)
            B.emit("vector", lambda e: e.tensor_mul(sl(1), sl(1), sl(8)),
                   inc=s_v)
            B.wait("vector", s_v)
            B.emit("vector", lambda e: e.tensor_mul(sl(1), sl(1), GT[:, 0:TH]),
                   inc=s_v)
            B.wait("vector", s_v)
            B.emit("vector", lambda e, g=g: e.tensor_mul(
                sl(5), sl(1), SB[:, g * TH:(g + 1) * TH]))
            B.emit("vector", lambda e, g=g: e.tensor_mul(
                sl(6), sl(1), SB[:, (8 + g) * TH:(9 + g) * TH]), inc=s_v)
            B.wait("vector", s_v)
            phi_init = 0.0 if th == 0 else CARRY[:, 3 * g:3 * g + 1]
            B.emit("vector", lambda e: e.memset(sl(8), 0.0), inc=s_v)
            B.wait("vector", s_v)
            B.emit("vector", lambda e, ini=phi_init: e.tensor_tensor_scan(
                sl(7), sl(2), sl(8), ini, ALU.add, ALU.add), inc=s_v)
            B.wait("vector", s_v)
            B.emit("vector", lambda e: e.tensor_scalar(
                sl(8), sl(7), INV_2PI, RND_C, ALU.mult, ALU.add), inc=s_v)
            B.wait("vector", s_v)
            B.emit("vector", lambda e: e.tensor_scalar(
                sl(8), sl(8), RND_C, None, ALU.subtract), inc=s_v)
            B.wait("vector", s_v)
            B.emit("vector", lambda e: e.scalar_tensor_tensor(
                sl(8), sl(8), -TWO_PI, sl(7), ALU.mult, ALU.add), inc=s_v)
            B.wait("vector", s_v)
            B.emit("vector", lambda e: e.tensor_scalar(
                sl(8), sl(8), PI_LO, -PI_LO, ALU.min, ALU.max), inc=s_v)
            B.wait("scalar", s_v)
            B.emit("scalar", lambda e: e.activation(sl(9), sl(8), AF.Sin),
                   inc=s_s)
            B.wait("vector", s_s)
            B.emit("vector", lambda e: e.tensor_scalar(
                sl(8), sl(7), INV_2PI, 0.25, ALU.mult, ALU.add), inc=s_v)
            B.wait("vector", s_v)
            B.emit("vector", lambda e: e.tensor_scalar(
                sl(8), sl(8), RND_C, RND_C, ALU.add, ALU.subtract), inc=s_v)
            B.wait("vector", s_v)
            B.emit("vector", lambda e: e.scalar_tensor_tensor(
                sl(8), sl(8), -TWO_PI, sl(7), ALU.mult, ALU.add), inc=s_v)
            B.wait("vector", s_v)
            B.emit("vector", lambda e: e.tensor_scalar(
                sl(8), sl(8), HALF_PI, None, ALU.add), inc=s_v)
            B.wait("vector", s_v)
            B.emit("vector", lambda e: e.tensor_scalar(
                sl(8), sl(8), PI_LO, -PI_LO, ALU.min, ALU.max), inc=s_v)
            B.wait("scalar", s_v)
            B.emit("scalar", lambda e: e.activation(sl(4), sl(8), AF.Sin),
                   inc=s_s)
            B.wait("vector", s_s)
            B.emit("vector", lambda e: e.tensor_mul(sl(8), sl(4), sl(5)))
            B.emit("vector", lambda e: e.tensor_mul(sl(2), sl(9), sl(6)),
                   inc=s_v)
            B.wait("vector", s_v)
            B.emit("vector", lambda e: e.tensor_add(sl(8), sl(8), sl(2)))
            B.emit("vector", lambda e: e.tensor_mul(sl(2), sl(4), sl(6)))
            B.emit("vector", lambda e: e.tensor_mul(sl(5), sl(9), sl(5)),
                   inc=s_v)
            B.wait("vector", s_v)
            B.emit("vector", lambda e: e.tensor_sub(sl(2), sl(2), sl(5)),
                   inc=s_v)
            B.wait("vector", s_v)
            gr_init = 0.0 if th == 0 else CARRY[:, 3 * g + 1:3 * g + 2]
            gi_init = 0.0 if th == 0 else CARRY[:, 3 * g + 2:3 * g + 3]
            B.emit("vector", lambda e, ini=gr_init: e.tensor_tensor_scan(
                sl(5), sl(3), sl(8), ini, ALU.mult, ALU.add))
            B.emit("vector", lambda e, ini=gi_init: e.tensor_tensor_scan(
                sl(6), sl(3), sl(2), ini, ALU.mult, ALU.add), inc=s_v)
            B.wait("vector", s_v)
            B.emit("vector", lambda e, g=g: e.tensor_copy(
                CARRY[:, 3 * g:3 * g + 1], sl(7)[:, TH - 1:TH]))
            B.emit("vector", lambda e, g=g: e.tensor_copy(
                CARRY[:, 3 * g + 1:3 * g + 2], sl(5)[:, TH - 1:TH]))
            B.emit("vector", lambda e, g=g: e.tensor_copy(
                CARRY[:, 3 * g + 2:3 * g + 3], sl(6)[:, TH - 1:TH]), inc=s_v)
            B.wait("vector", s_v)
            B.emit("vector", lambda e: e.tensor_mul(sl(8), sl(4), sl(5)))
            B.emit("vector", lambda e: e.tensor_mul(sl(2), sl(9), sl(6)),
                   inc=s_v)
            B.wait("vector", s_v)
            B.emit("vector", lambda e, g=g: e.tensor_sub(
                SE[:, (2 * g) * TH:(2 * g + 1) * TH], sl(8), sl(2)))
            B.emit("vector", lambda e: e.tensor_mul(sl(8), sl(9), sl(5)))
            B.emit("vector", lambda e: e.tensor_mul(sl(2), sl(4), sl(6)),
                   inc=s_v)
            B.wait("vector", s_v)
            B.emit("vector", lambda e, g=g: e.tensor_add(
                SE[:, (2 * g + 1) * TH:(2 * g + 2) * TH], sl(8), sl(2)),
                inc=s_v)
            B.wait("vector", s_v)
        H_OK = s_v.n

        def evac_y(m0, mw, p, tv):
            k = m0 // 128
            B.wait("vector", s_t, tv)
            v = B.emit("vector", lambda e, p=p, k=k: e.scalar_tensor_tensor(
                SA[:, k * TH:(k + 1) * TH], p, ccol(C_BOUT + k),
                SA[:, k * TH:(k + 1) * TH], ALU.add, ALU.add), inc=s_v)
            return (s_v, v)

        gemm(wout_bf, NCH, D, lambda k: SE[:, k * TH:(k + 1) * TH], evac_y,
             rhs_ready=(s_v, H_OK), cast_ready=CAST["wout"])
        X1_OK = s_v.n
        B.wait("sync", s_v, X1_OK)
        for k in range(NCH):
            B.dma("sync", x1_d[k * 128:(k + 1) * 128, tok],
                  SA[:, k * TH:(k + 1) * TH], s_st)
            if debug:
                B.dma("sync", dbg["x1"][k * 128:(k + 1) * 128, tok],
                      SA[:, k * TH:(k + 1) * TH], s_st)

        def extra_rows(rows):
            var = rows["var"]
            rstd = rows["rstd"]
            r2 = SPV[0:1, 5 * TH:6 * TH]
            B.emit("vector", lambda e: e.tensor_mul(r2, rstd, rstd), inc=s_v)
            B.wait("vector", s_v)
            B.emit("vector", lambda e: e.tensor_mul(r2, r2, var), inc=s_v)
            B.wait("scalar", s_v)
            B.emit("scalar", lambda e: e.activation(r2, r2, AF.Ln, bias=ccol(C_EPS)[0:1, :]))
            B.emit("scalar", lambda e: e.activation(r2, r2, AF.Exp,
                                                    scale=-0.5), inc=s_s)
            B.wait("vector", s_s)

        ln2_ms = norm(lambda k: SA[:, k * TH:(k + 1) * TH], "ln",
                      out_fn=lambda k: SB[:, k * TH:(k + 1) * TH],
                      extra=extra_rows)
        if th >= 2:
            B.wait("sync", s_v, ln2_ms)
            otok = slice((th - 2) * TH, (th - 1) * TH)
            for k in range(NCH):
                B.dma("sync", xo_d[k * 128:(k + 1) * 128, otok],
                      SB[:, k * TH:(k + 1) * TH], s_st)
        if guard[1] is not None:
            B.wait("tensor", *guard[1])
            guard[1] = None
        B.wait("tensor", s_v, ln2_ms)
        B.emit("tensor", lambda e: e.matmul(
            PD[:, 0:TH], ONER, SPV[0:1, 5 * TH:6 * TH], start=True,
            stop=True), inc=s_t)
        B.wait("vector", s_t)
        for k in range(NCH):
            B.emit("vector", lambda e, k=k: e.scalar_tensor_tensor(
                SF[:, k * TH:(k + 1) * TH], SB[:, k * TH:(k + 1) * TH],
                ccol(C_LN1 + k), PD[:, 0:TH], ALU.mult, ALU.mult), inc=s_v)
        RMS1_OK = s_v.n
        B.emit("vector", lambda e: e.nop(), inc=s_v)
        guard[1] = (s_v, s_v.n)

        def copy_evac(base):
            def evac(m0, mw, p, tv):
                B.wait("vector", s_t, tv)
                v = B.emit("vector", lambda e, p=p, d=SD[
                    :, (base + m0 // 128) * TH:(base + m0 // 128 + 1) * TH]:
                    e.tensor_copy(d, p), inc=s_v)
                return (s_v, v)
            return evac

        gemm(kvd_bf, NCH, LAT, lambda k: SF[:, k * TH:(k + 1) * TH],
             copy_evac(0), rhs_ready=(s_v, RMS1_OK), cast_ready=CAST["proj"])
        if th >= 2:
            gemm(qd_bf, NCH, LAT, lambda k: SF[:, k * TH:(k + 1) * TH],
                 copy_evac(4))
        LAT_OK = s_v.n

        def stage_k_evac(colbase):
            def evac(m0, mw, p, tv):
                hp = m0 // 128
                par = hp % 2
                tile = QKT[:, (2 + par) * 512:(3 + par) * 512]
                B.wait("vector", s_t, tv)
                B.wait("vector", s_lg)
                v = B.emit("vector", lambda e, p=p, t=tile: e.tensor_copy(
                    t, p), inc=s_v)
                B.wait("gpsimd", s_v, v)
                B.dma("gpsimd", k_d[(2 * hp) * 128 + colbase:
                                    (2 * hp) * 128 + colbase + 64, tok],
                      tile[0:64, :], s_lg)
                B.dma("gpsimd", k_d[(2 * hp + 1) * 128 + colbase:
                                    (2 * hp + 1) * 128 + colbase + 64, tok],
                      tile[64:128, :], s_lg)
                return (s_v, v)
            return evac

        gemm(ku_bf, LCH, DH, lambda k: SD[:, k * TH:(k + 1) * TH],
             stage_k_evac(0), rhs_ready=(s_v, LAT_OK))

        if th >= 2:
            otok = slice((th - 2) * TH, (th - 1) * TH)

            def stage_q_evac(colbase):
                def evac(m0, mw, p, tv):
                    hp = m0 // 128
                    par = hp % 2
                    tile = QKT[:, (2 + par) * 512:(3 + par) * 512]
                    B.wait("vector", s_t, tv)
                    B.wait("vector", s_lg)
                    v = B.emit("vector", lambda e, p=p, t=tile:
                               e.tensor_scalar_mul(t, p, QSCALE), inc=s_v)
                    B.wait("gpsimd", s_v, v)
                    B.dma("gpsimd", q_dram[(2 * hp) * 128 + colbase:
                                           (2 * hp) * 128 + colbase + 64,
                                           otok], tile[0:64, :], s_lg)
                    B.dma("gpsimd", q_dram[(2 * hp + 1) * 128 + colbase:
                                           (2 * hp + 1) * 128 + colbase + 64,
                                           otok], tile[64:128, :], s_lg)
                    return (s_v, v)
                return evac

            gemm(qu_bf, LCH, DH, lambda k: SD[:, (4 + k) * TH:(5 + k) * TH],
                 stage_q_evac(0))

        B.wait("tensor", s_v, LAT_OK)
        for vb in range(4):
            pbank = (PA, PB) if vb % 2 == 0 else (PC, PD)
            if guard[vb % 2] is not None:
                B.wait("tensor", *guard[vb % 2])
                guard[vb % 2] = None
            for lat in range(LCH):
                si = (vb * LCH + lat) % 4
                if bslab_guard[si] is not None:
                    B.wait("sync", *bslab_guard[si])
                wv = B.dma("sync", WBUFB[:, si * 512:(si + 1) * 512],
                           vu_bf[lat * 128:(lat + 1) * 128,
                                 vb * 512:(vb + 1) * 512], s_sl[si])
                B.wait("tensor", s_sl[si], wv)
                tvv = None
                for tc in range(4):
                    pt = pbank[tc // 2][:, (tc % 2) * 512:(tc % 2) * 512 + 512]
                    tvv = B.emit("tensor", lambda e, p=pt,
                                 w=SD[:, lat * TH + tc * 128:
                                      lat * TH + tc * 128 + 128],
                                 r=WBUFB[:, si * 512:(si + 1) * 512],
                                 st=(lat == 0), sp=(lat == LCH - 1):
                                 e.matmul(p, w, r, start=st, stop=sp),
                                 inc=(s_t if tc == 3 else None))
                bslab_guard[si] = (s_t, tvv)
            VMM = s_t.n
            last = None
            for tc in range(4):
                pt = pbank[tc // 2][:, (tc % 2) * 512:(tc % 2) * 512 + 512]
                par = tc % 2
                tile = QKT[:, (2 + par) * 512:(3 + par) * 512]
                B.wait("vector", s_t, VMM)
                B.wait("vector", s_lg)
                v = B.emit("vector", lambda e, p=pt, t=tile: e.tensor_copy(
                    t, p), inc=s_v)
                B.wait("gpsimd", s_v, v)
                for j in range(4):
                    h = vb * 4 + j
                    B.dma("gpsimd",
                          v_d[h * T + th * TH + tc * 128:
                              h * T + th * TH + tc * 128 + 128, :],
                          tile[:, j * 128:(j + 1) * 128], s_lg)
                last = (s_v, v)
            guard[vb % 2] = last

        def load_rope_tabs(ctab, stab):
            B.wait("scalar", s_v)
            B.dma("scalar", SPV[:, 8 * TH:9 * TH], ctab[:, tok], s_ls)
            B.dma("scalar", SPV[:, 9 * TH:10 * TH], stab[:, tok], s_ls)
            return s_ls.n

        def mk_rope_evac(dst_rows, toksl):
            def evac(m0, mw, p, tv):
                hp = m0 // 128
                rt = SPV[:, 6 * TH:7 * TH]
                sh = SPV[:, 7 * TH:8 * TH]
                B.wait("vector", s_t, tv)
                B.wait("vector", s_lg)
                v0 = B.emit("vector", lambda e, p=p: e.tensor_copy(rt, p),
                            inc=s_v)
                B.wait("gpsimd", s_v, v0)
                B.dma("gpsimd", sh[0:32, :], rt[32:64, :], s_lg)
                B.dma("gpsimd", sh[32:64, :], rt[0:32, :], s_lg)
                B.dma("gpsimd", sh[64:96, :], rt[96:128, :], s_lg)
                B.dma("gpsimd", sh[96:128, :], rt[64:96, :], s_lg)
                B.wait("vector", s_lg)
                B.emit("vector", lambda e: e.tensor_mul(
                    rt, rt, SPV[:, 8 * TH:9 * TH]))
                B.emit("vector", lambda e: e.tensor_mul(
                    sh, sh, SPV[:, 9 * TH:10 * TH]), inc=s_v)
                B.wait("vector", s_v)
                tile = QKT[:, 2 * 512:3 * 512]
                B.wait("vector", s_ls)
                v = B.emit("vector", lambda e: e.tensor_add(tile, rt, sh),
                           inc=s_v)
                B.wait("gpsimd", s_v, v)
                B.dma("gpsimd", dst_rows((2 * hp) * 128 + 64,
                                         (2 * hp) * 128 + 128, toksl),
                      tile[0:64, :], s_lg)
                B.dma("gpsimd", dst_rows((2 * hp + 1) * 128 + 64,
                                         (2 * hp + 1) * 128 + 128, toksl),
                      tile[64:128, :], s_lg)
                B.wait("tensor", s_v, v)
                return (s_v, v)
            return evac

        load_rope_tabs(cosk_t, sinkS_t)
        B.wait("vector", s_ls)
        gemm(ropek_bf, NCH, DH, lambda k: SF[:, k * TH:(k + 1) * TH],
             mk_rope_evac(lambda a, b, ts: k_d[a:b, ts], tok))
        if th >= 2:
            load_rope_tabs(cosq_t, sinqS_t)
            B.wait("vector", s_ls)
            gemm(ropeq_bf, LCH, DH,
                 lambda k: SD[:, (4 + k) * TH:(5 + k) * TH],
                 mk_rope_evac(lambda a, b, ts: q_dram[a:b, ts], otok))


    for th in range(NTH):
        stage_b(th)

    B.dma("gpsimd", ow_bf, o_w, CSEM["ow"])
    CAST["ow"] = ("ow", CSEM["ow"].n)
    for e in range(8):
        B.dma("gpsimd", g_bf[e], sh_gate if e == 0 else r_gate[e - 1],
              CSEM["moe"])
        B.dma("gpsimd", u_bf[e], sh_up if e == 0 else r_up[e - 1],
              CSEM["moe"])
        B.dma("gpsimd", d_bf[e], sh_down if e == 0 else r_down[e - 1],
              CSEM["moe"])
    CAST["moe"] = ("moe", CSEM["moe"].n)

    def mh_pass2():
        barrier()
        for which, src, ntile, grid_col in (
                ("k", k_d, NTH, 0), ("q", q_dram, 2, 1)):
            for h in range(NH):
                acc = MROW[0:1, grid_col * NH + h:grid_col * NH + h + 1]
                for t2 in range(ntile):
                    tile = QKT[:, 2 * 512:3 * 512]
                    B.wait("scalar", s_v)
                    B.wait("scalar", s_t)
                    lv = B.dma("scalar", tile,
                               src[h * 128:(h + 1) * 128,
                                   t2 * 512:(t2 + 1) * 512], s_ls)
                    B.wait("vector", s_ls, lv)
                    B.emit("vector", lambda e, t=tile: e.tensor_mul(
                        SQT[:, 0:TH], t, t), inc=s_v)
                    B.wait("tensor", s_v)
                    B.emit("tensor", lambda e: e.matmul(
                        PD[0:1, 0:TH], ONECR, SQT[:, 0:TH], start=True,
                        stop=True), inc=s_t)
                    B.wait("vector", s_t)
                    B.emit("vector", lambda e: e.reduce_max(
                        SPV[0:1, 0:1], PD[0:1, 0:TH], axis=AX.X), inc=s_v)
                    B.wait("vector", s_v)
                    B.emit("vector", lambda e, a=acc: e.tensor_max(
                        a, a, SPV[0:1, 0:1]), inc=s_v)
                    B.wait("vector", s_v)
        B.emit("vector", lambda e: e.tensor_mul(
            MROW[0:1, 2 * NH:3 * NH], MROW[0:1, 0:NH], MROW[0:1, NH:2 * NH]),
            inc=s_v)
        B.wait("scalar", s_v)
        B.emit("scalar", lambda e: e.activation(
            MROW[0:1, 2 * NH:3 * NH], MROW[0:1, 2 * NH:3 * NH], AF.Sqrt),
            inc=s_s)
        B.wait("vector", s_s)
        B.emit("vector", lambda e: e.tensor_scalar_mul(
            MROW[0:1, 2 * NH:3 * NH], MROW[0:1, 2 * NH:3 * NH], -1.0001),
            inc=s_v)
        B.wait("tensor", s_v)
        for h in range(NH):
            if guard[1] is not None:
                B.wait("tensor", *guard[1])
                guard[1] = None
            B.emit("tensor", lambda e, h=h: e.matmul(
                PD[:, 0:1], ONER, MROW[0:1, 2 * NH + h:2 * NH + h + 1],
                start=True, stop=True), inc=s_t)
            B.wait("vector", s_t)
            B.emit("vector", lambda e, h=h: e.tensor_copy(
                ccol(C_MH + h), PD[:, 0:1]), inc=s_v)
            B.wait("vector", s_v)
            B.emit("vector", lambda e, h=h: e.tensor_add(
                ccol(C_MHX + h), ccol(C_MH + h), ccol(C_KTB)), inc=s_v)
            B.emit("vector", lambda e: e.nop(), inc=s_v)
            guard[1] = (s_v, s_v.n)

    if phase == "B":
        barrier()
        B.wait("sync", s_st)
        for k in range(NCH):
            B.dma("sync", out_d[k * 128:(k + 1) * 128, :],
                  x1_d.bitcast(F32)[k * 128:(k + 1) * 128, TOFF:T], s_st)
        B.wait("sync", s_st)
        B.build()
        return nc

    mh_pass2()

    def attn_mix(qi):
        qtok = slice(qi * TH, (qi + 1) * TH)
        kt_max = (TOFF + (qi + 1) * TH) // 128
        s_q = [None, None]
        pv_guard = [None, None]
        exp_done = {}
        pv_done = {}
        barrier()
        if qi == 0:
            for j in range(4):
                B.dma("scalar", QKT[:, 3584 + j * 512:3584 + (j + 1) * 512],
                      mask_t[(8 + j) * 128:(9 + j) * 128, 0:512], s_ls)
            B.wait("vector", s_ls)

        for h in range(NH):
            par = h % 2
            qtile = QKT[:, 2560 + par * 512:2560 + (par + 1) * 512]
            if pv_guard[par] is not None:
                B.wait("sync", *pv_guard[par])
                B.wait("tensor", *pv_guard[par])
            qv = B.dma("sync", qtile, q_dram[h * 128:(h + 1) * 128, qtok],
                       s_qt2[par])
            B.wait("tensor", s_qt2[par], qv)
            pvp = PB[:, par * 512:par * 512 + 512]
            denp = PC[:, par * 512:par * 512 + 512]
            for kt in range(kt_max):
                kpar = kt % 2
                kb = kpar * 1280
                ktile = QKT[:, kb:kb + 128]
                vtile = QKT[:, kb + 128:kb + 256]
                etile = QKT[:, kb + 768:kb + 1280]
                key = (h, kt - 2)
                if kt < 2 and h > 0:
                    ktp = kt_max - 1 if (kt_max - 1) % 2 == kpar \
                        else kt_max - 2
                    key = (h - 1, ktp)
                if key in pv_done:
                    B.wait("sync", s_t, pv_done[key])
                B.dma("sync", ktile,
                      k_d[h * 128:(h + 1) * 128, kt * 128:(kt + 1) * 128],
                      s_kv2[kpar])
                lv = B.dma("sync", vtile,
                           v_d[h * T + kt * 128:h * T + (kt + 1) * 128, :],
                           s_kv2[kpar])
                sp_ = PA[:, kpar * 512:kpar * 512 + 512]
                if (h, kt - 2) in exp_done:
                    B.wait("tensor", s_s, exp_done[(h, kt - 2)])
                B.wait("tensor", s_kv2[kpar], lv)
                tv = B.emit("tensor", lambda e, p=sp_, w=ktile, r=qtile:
                            e.matmul(p, w, r, start=True, stop=True),
                            inc=s_t)
                B.wait("scalar", s_t, tv)
                bc = C_MHX + h if kt < 8 else C_MH + h
                ev = B.emit("scalar", lambda e, p=sp_, o=etile, bc=bc:
                            e.activation(o, p, AF.Exp, bias=ccol(bc)),
                            inc=s_s)
                exp_done[(h, kt)] = ev
                dj = kt - (8 + 4 * qi)
                if dj >= 0:
                    B.wait("vector", s_s, ev)
                    vv = B.emit("vector", lambda e, o=etile, j=dj:
                                e.tensor_tensor(
                                    o, o, QKT[:, 3584 + j * 512:
                                              3584 + (j + 1) * 512],
                                    ALU.mult), inc=s_v)
                    B.wait("tensor", s_v, vv)
                else:
                    B.wait("tensor", s_s, ev)
                B.emit("tensor", lambda e, p=pvp, w=vtile, r=etile,
                       st=(kt == 0), sp=(kt == kt_max - 1):
                       e.matmul(p, w, r, start=st, stop=sp))
                tv2 = B.emit("tensor", lambda e, p=denp, r=etile,
                             st=(kt == 0), sp=(kt == kt_max - 1):
                             e.matmul(p, ONEBF, r, start=st, stop=sp),
                             inc=s_t)
                pv_done[(h, kt)] = tv2
            B.wait("vector", s_t, pv_done[(h, kt_max - 1)])
            rt_ = SPV[:, (4 + par) * TH:(5 + par) * TH]
            rv = B.emit("vector", lambda e, d=denp, r=rt_: e.reciprocal(
                r, d), inc=s_v)
            B.wait("vector", s_v, rv)
            ov2 = B.emit("vector", lambda e, p=pvp, r=rt_,
                         d=SD[:, h * TH:(h + 1) * TH]: e.tensor_tensor(
                d, p, r, ALU.mult), inc=s_v)
            pv_guard[par] = (s_v, ov2)
        barrier(["tensor", "vector", "scalar", "sync"])

        for k in range(NCH):
            B.dma("sync", SB[:, k * TH:(k + 1) * TH],
                  xo_d[k * 128:(k + 1) * 128, qtok], s_x)
        XO_LD = s_x.n
        B.wait("vector", s_x, XO_LD)

        def evac_h1(m0, mw, p, tv):
            k = m0 // 128
            B.wait("vector", s_t, tv)
            v = B.emit("vector", lambda e, p=p, k=k: e.tensor_add(
                SB[:, k * TH:(k + 1) * TH], SB[:, k * TH:(k + 1) * TH], p),
                inc=s_v)
            return (s_v, v)

        gemm(ow_bf, NCH, D, lambda k: SD[:, k * TH:(k + 1) * TH], evac_h1,
             cast_ready=CAST["ow"])
        H1_OK = s_v.n
        B.wait("sync", s_v, H1_OK)
        for k in range(NCH):
            B.dma("sync", h1_d[k * 128:(k + 1) * 128, qtok],
                  SB[:, k * TH:(k + 1) * TH], s_st)
            if debug:
                B.dma("sync", dbg["h1"][k * 128:(k + 1) * 128, qtok],
                      SB[:, k * TH:(k + 1) * TH], s_st)

        B.wait("tensor", s_v, H1_OK)
        if guard[1] is not None:
            B.wait("tensor", *guard[1])
            guard[1] = None
        sq_r = PC[0:1, 512:512 + TH]
        sqv2 = {}
        for k in range(NCH):
            if k >= 1:
                B.wait("vector", s_t, sqv2[k - 1])
            B.emit("vector", lambda e, k=k: e.tensor_mul(
                SQT[:, 0:TH], SB[:, k * TH:(k + 1) * TH],
                SB[:, k * TH:(k + 1) * TH]), inc=s_v)
            B.wait("tensor", s_v)
            sqv2[k] = B.emit("tensor",
                             lambda e, st=(k == 0), sp=(k == NCH - 1):
                             e.matmul(sq_r, ONECR, SQT[:, 0:TH],
                                      start=st, stop=sp), inc=s_t)
        B.wait("vector", s_t)
        r = SPV[0:1, (8 + qi) * TH:(9 + qi) * TH]
        B.emit("vector", lambda e: e.tensor_scalar_mul(r, sq_r, 1.0 / D),
               inc=s_v)
        B.wait("scalar", s_v)
        B.emit("scalar", lambda e: e.activation(r, r, AF.Ln,
                                                bias=ccol(C_EPS)[0:1, :]))
        B.emit("scalar", lambda e: e.activation(r, r, AF.Exp, scale=-0.5),
               inc=s_s)
        B.wait("vector", s_s)
        B.emit("vector", lambda e: e.nop(), inc=s_v)
        guard[1] = (s_v, s_v.n)
        if guard[1] is not None:
            B.wait("tensor", *guard[1])
            guard[1] = None
        B.emit("tensor", lambda e: e.matmul(
            PD[:, 0:TH], ONER, SPV[0:1, (8 + qi) * TH:(9 + qi) * TH],
            start=True, stop=True), inc=s_t)
        B.wait("vector", s_t)
        for k in range(NCH):
            B.emit("vector", lambda e, k=k: e.scalar_tensor_tensor(
                SF[:, k * 1024 + qi * TH:k * 1024 + (qi + 1) * TH],
                SB[:, k * TH:(k + 1) * TH], ccol(C_LN2 + k), PD[:, 0:TH],
                ALU.mult, ALU.mult), inc=s_v)
        RMS2_OK = s_v.n
        B.emit("vector", lambda e: e.nop(), inc=s_v)
        guard[1] = (s_v, s_v.n)

        for tc in range(4):
            if guard[0] is not None:
                B.wait("tensor", *guard[0])
                guard[0] = None
            lgmm = {}
            for k in range(NCH):
                w2 = WL2[:, (k % 2) * NR:(k % 2 + 1) * NR]
                if k >= 2:
                    B.wait("vector", s_t, lgmm[k - 2])
                wlv = B.emit("vector", lambda e, k=k, w2=w2: e.tensor_scalar(
                    w2, RW[:, k * NR:(k + 1) * NR], ccol(C_LN2 + k), None,
                    ALU.mult), inc=s_v)
                B.wait("tensor", s_v, wlv)
                lgmm[k] = B.emit("tensor", lambda e, k=k, tc=tc, w2=w2,
                                 st=(k == 0), sp=(k == NCH - 1): e.matmul(
                    PA[:, 0:NR],
                    SB[:, k * TH + tc * 128:k * TH + (tc + 1) * 128],
                    w2, start=st, stop=sp), inc=s_t)
            LG_MM = s_t.n
            if guard[1] is not None:
                B.wait("tensor", *guard[1])
                guard[1] = None
            B.emit("tensor", lambda e, tc=tc: e.matmul(
                PD[:, 1:2],
                SPV[0:1, (8 + qi) * TH + tc * 128:(8 + qi) * TH + tc * 128 + 128],
                ONE1, start=True, stop=True), inc=s_t)
            B.wait("vector", s_t)
            lgt = SPV[:, 4 * TH:4 * TH + NR]
            B.emit("vector", lambda e: e.tensor_copy(
                SPV[:, 5 * TH:5 * TH + 1], PD[:, 1:2]), inc=s_v)
            B.wait("vector", s_v)
            guard[1] = (s_v, s_v.n)
            B.emit("vector", lambda e: e.scalar_tensor_tensor(
                lgt, PA[:, 0:NR], SPV[:, 5 * TH:5 * TH + 1], RB,
                ALU.mult, ALU.add), inc=s_v)
            B.wait("vector", s_v)
            guard[0] = (s_v, s_v.n)
            m1 = SPV[:, 5 * TH + 4:5 * TH + 5]
            m2 = SPV[:, 5 * TH + 5:5 * TH + 6]
            msk = SPV[:, 5 * TH + 8:5 * TH + 8 + NR]
            prb = SPV[:, 6 * TH:6 * TH + NR]
            B.emit("vector", lambda e: e.reduce_max(m1, lgt, axis=AX.X),
                   inc=s_v)
            B.wait("vector", s_v)
            B.emit("vector", lambda e: e.tensor_scalar(
                msk, lgt, m1, None, ALU.is_ge), inc=s_v)
            B.wait("vector", s_v)
            B.emit("vector", lambda e: e.scalar_tensor_tensor(
                msk, msk, -1e30, lgt, ALU.mult, ALU.add), inc=s_v)
            B.wait("vector", s_v)
            B.emit("vector", lambda e: e.reduce_max(m2, msk, axis=AX.X),
                   inc=s_v)
            B.wait("vector", s_v)
            B.emit("vector", lambda e: e.tensor_scalar(
                msk, lgt, m2, None, ALU.is_ge), inc=s_v)
            B.wait("scalar", s_v)
            B.emit("scalar", lambda e: e.activation(prb, lgt, AF.Sigmoid),
                   inc=s_s)
            B.wait("vector", s_s)
            B.emit("vector", lambda e: e.tensor_mul(msk, msk, prb), inc=s_v)
            B.wait("vector", s_v)
            B.emit("vector", lambda e: e.tensor_reduce(
                m1, msk, axis=AX.X, op=ALU.add), inc=s_v)
            B.wait("vector", s_v)
            B.emit("vector", lambda e: e.reciprocal(m2, m1), inc=s_v)
            B.wait("vector", s_v)
            B.emit("vector", lambda e, qi=qi, tc=tc: e.tensor_scalar(
                WTOK[:, (qi * 4 + tc) * NR:(qi * 4 + tc + 1) * NR],
                msk, m2, None, ALU.mult), inc=s_v)
            B.wait("vector", s_v)
        if debug:
            B.wait("sync", s_v)
            B.dma("sync", dbg["w"], WTOK, s_st)

    for qi in range(2):
        attn_mix(qi)

    if phase == "BC":
        barrier()
        for k in range(NCH):
            B.dma("sync", out_d[k * 128:(k + 1) * 128, :],
                  h1_d[k * 128:(k + 1) * 128, :], s_st)
        B.wait("sync", s_st)
        B.build()
        return nc

    barrier()
    B.wait("sync", CSEM[CAST["moe"][0]], CAST["moe"][1])
    OUT = {0: SA, 1: SB}
    for k in range(NCH):
        B.dma("sync", SA[:, k * TH:(k + 1) * TH],
              x1_d[k * 128:(k + 1) * 128, TOFF:TOFF + TH], s_x)
    B.wait("vector", s_x)
    for k in range(NCH):
        lv = B.dma("scalar", SPV[:, 4 * TH:5 * TH],
                   h1_d[k * 128:(k + 1) * 128, 0:TH], s_ls)
        B.wait("vector", s_ls, lv)
        v = B.emit("vector", lambda e, k=k: e.tensor_add(
            SA[:, k * TH:(k + 1) * TH], SA[:, k * TH:(k + 1) * TH],
            SPV[:, 4 * TH:5 * TH]), inc=s_v)
        B.wait("scalar", s_v, v)
    for k in range(NCH):
        lv = B.dma("scalar", SPV[:, 4 * TH:5 * TH],
                   x1_d.bitcast(F32)[k * 128:(k + 1) * 128,
                                     TOFF + TH:TOFF + 2 * TH], s_ls)
        B.wait("vector", s_ls, lv)
        v = B.emit("vector", lambda e, k=k: e.tensor_add(
            SB[:, k * TH:(k + 1) * TH], SB[:, k * TH:(k + 1) * TH],
            SPV[:, 4 * TH:5 * TH]), inc=s_v)
        B.wait("scalar", s_v, v)
    barrier()

    for ei in range(8):
        routed = ei > 0
        er = ei - 1
        wg_d, wu_d, wd_d = g_bf[ei], u_bf[ei], d_bf[ei]
        if routed:
            B.wait("gpsimd", s_v)
            if guard[1] is not None:
                B.wait("tensor", *guard[1])
                guard[1] = None
            for tcq in range(8):
                B.dma("gpsimd", SPV[0:1, TOFF + tcq * 128:
                                    TOFF + (tcq + 1) * 128],
                      WTOK[:, tcq * NR + er:tcq * NR + er + 1], s_lg)
            B.wait("vector", s_lg)
            B.emit("vector", lambda e: e.tensor_scalar_mul(
                SPV[0:1, 0:TOFF], SPV[0:1, TOFF:2 * TOFF], 0.125), inc=s_v)
            B.wait("tensor", s_v)
            for qi in range(2):
                B.emit("tensor", lambda e, qi=qi: e.matmul(
                    PD[:, qi * 512:(qi + 1) * 512], ONER,
                    SPV[0:1, qi * 512:(qi + 1) * 512], start=True,
                    stop=True), inc=(s_t if qi == 1 else None))
            B.wait("vector", s_t)
            for qi in range(2):
                B.emit("vector", lambda e, qi=qi: e.tensor_copy(
                    SPV[:, (6 + qi) * TH:(7 + qi) * TH],
                    PD[:, qi * 512:(qi + 1) * 512]), inc=s_v)
            B.wait("tensor", s_v)
            B.emit("vector", lambda e: e.nop(), inc=s_v)
            guard[1] = (s_v, s_v.n)
        WB_OK = s_v.n

        for qi in range(2):
            wbt = SPV[:, (6 + qi) * TH:(7 + qi) * TH]
            for ib in range(2):
                for wsel, wdram, pb in ((0, wg_d, (PA, PB)),
                                        (1, wu_d, (PC, PD))):
                    if guard[wsel] is not None:
                        B.wait("tensor", *guard[wsel])
                        guard[wsel] = None
                    for k in range(NCH):
                        si = k % 4
                        if bslab_guard[si] is not None:
                            B.wait("sync", *bslab_guard[si])
                        wv = B.dma("sync",
                                   WBUFB[:, si * 512:si * 512 + 512],
                                   wdram[k * 128:(k + 1) * 128,
                                         ib * 512:(ib + 1) * 512], s_sl[si])
                        B.wait("tensor", s_sl[si], wv)
                        tv = None
                        for mc in range(4):
                            pt = pb[mc // 2][:, (mc % 2) * 512:
                                             (mc % 2) * 512 + 512]
                            tv = B.emit(
                                "tensor",
                                lambda e, p=pt,
                                w=WBUFB[:, si * 512 + mc * 128:
                                        si * 512 + (mc + 1) * 128],
                                r=SF[:, k * 1024 + qi * TH:
                                     k * 1024 + (qi + 1) * TH],
                                st=(k == 0), sp=(k == NCH - 1):
                                e.matmul(p, w, r, start=st, stop=sp),
                                inc=(s_t if mc == 3 else None))
                        bslab_guard[si] = (s_t, tv)
                    guard[wsel] = (s_t, s_t.n)
                GU = s_t.n
                last = None
                for mc in range(4):
                    gp = (PA, PB)[mc // 2][:, (mc % 2) * 512:
                                           (mc % 2) * 512 + 512]
                    up = (PC, PD)[mc // 2][:, (mc % 2) * 512:
                                           (mc % 2) * 512 + 512]
                    sg = SPV[:, 4 * TH:5 * TH]
                    B.wait("scalar", s_t, GU)
                    B.wait("scalar", s_v)
                    sv = B.emit("scalar", lambda e, g=gp, s=sg: e.activation(
                        s, g, AF.Silu), inc=s_s)
                    B.wait("vector", s_s, sv)
                    hv = B.emit("vector", lambda e, s=sg, u=up: e.tensor_tensor(
                        SPV[:, 5 * TH:6 * TH], s, u, ALU.mult), inc=s_v)
                    B.wait("vector", s_v, hv)
                    ic = ib * 4 + mc
                    if routed:
                        hv2 = B.emit("vector", lambda e, ic=ic, qi=qi, w=wbt:
                                     e.tensor_tensor(
                                         SE[:, ic * 1024 + qi * TH:
                                            ic * 1024 + (qi + 1) * TH],
                                         SPV[:, 5 * TH:6 * TH], w, ALU.mult),
                                     inc=s_v)
                    else:
                        hv2 = B.emit("vector", lambda e, ic=ic, qi=qi:
                                     e.tensor_scalar_mul(
                                         SE[:, ic * 1024 + qi * TH:
                                            ic * 1024 + (qi + 1) * TH],
                                         SPV[:, 5 * TH:6 * TH], 0.125),
                                     inc=s_v)
                    last = (s_v, hv2)
                guard[0] = last
                guard[1] = last
            HID_OK = s_v.n

            for mb in range(4):
                pbank = (PA, PB) if mb % 2 == 0 else (PC, PD)
                if guard[mb % 2] is not None:
                    B.wait("tensor", *guard[mb % 2])
                    guard[mb % 2] = None
                for k in range(ICH):
                    si = k % 4
                    if bslab_guard[si] is not None:
                        B.wait("sync", *bslab_guard[si])
                    wv = B.dma("sync", WBUFB[:, si * 512:si * 512 + 512],
                               wd_d[k * 128:(k + 1) * 128,
                                    mb * 512:(mb + 1) * 512], s_sl[si])
                    B.wait("tensor", s_sl[si], wv)
                    if k == 0 and mb == 0:
                        B.wait("tensor", s_v, HID_OK)
                    tv = None
                    for mc in range(4):
                        pt = pbank[mc // 2][:, (mc % 2) * 512:
                                            (mc % 2) * 512 + 512]
                        tv = B.emit(
                            "tensor",
                            lambda e, p=pt,
                            w=WBUFB[:, si * 512 + mc * 128:
                                    si * 512 + (mc + 1) * 128],
                            r=SE[:, k * 1024 + qi * TH:
                                 k * 1024 + (qi + 1) * TH],
                            st=(k == 0), sp=(k == ICH - 1):
                            e.matmul(p, w, r, start=st, stop=sp),
                            inc=(s_t if mc == 3 else None))
                    bslab_guard[si] = (s_t, tv)
                DN = s_t.n
                last = None
                for mc in range(4):
                    pt = pbank[mc // 2][:, (mc % 2) * 512:(mc % 2) * 512 + 512]
                    kk = mb * 4 + mc
                    B.wait("vector", s_t, DN)
                    last = B.emit("vector", lambda e, p=pt, kk=kk, qi=qi:
                                  e.tensor_add(OUT[qi][:, kk * TH:(kk + 1) * TH],
                                               OUT[qi][:, kk * TH:(kk + 1) * TH],
                                               p), inc=s_v)
                    last = (s_v, last)
                guard[mb % 2] = last

    barrier()
    for k in range(NCH):
        B.dma("sync", out_d[k * 128:(k + 1) * 128, 0:TH].bitcast(F32R),
              SA[:, k * TH:(k + 1) * TH], s_st)
        B.dma("sync", out_d[k * 128:(k + 1) * 128, TH:2 * TH],
              SB[:, k * TH:(k + 1) * TH], s_st)
    B.wait("sync", s_st)
    barrier()
    B.build()
    return nc


def make_inputs(inp):
    x = np.ascontiguousarray(inp["L0_attn_x"], dtype=np.float32)
    wout = np.asarray(inp["L0_attn_W_out"], dtype=np.float32)
    perm = []
    for g in range(G):
        perm.extend(range(g * 128, (g + 1) * 128))
        perm.extend(range((8 + g) * 128, (9 + g) * 128))
    wout_p = np.ascontiguousarray(wout[np.asarray(perm)])
    shared = {
        "W_in": inp["L0_attn_W_in"],
        "Wg_x": inp["L0_attn_Wg_x"],
        "Wg_a": inp["L0_attn_Wg_a"],
        "W_out": wout_p,
        "b_in": inp["L0_attn_b_in"],
        "bg_x": inp["L0_attn_bg_x"],
        "bg_a": inp["L0_attn_bg_a"],
        "b_out": inp["L0_attn_b_out"],
        "a_real": inp["L0_attn_a_real_param"],
        "a_imag": inp["L0_attn_a_imag_param"],
        "kv_d_w": inp["L0_ffn_kv_d_w"],
        "q_d_w": inp["L0_ffn_q_d_w"],
        "k_u_w": inp["L0_ffn_k_u_w"],
        "v_u_w": inp["L0_ffn_v_u_w"],
        "q_u_w": inp["L0_ffn_q_u_w"],
        "rope_k_w": inp["L0_ffn_rope_k_w"],
        "rope_q_w": inp["L0_ffn_rope_q_w"],
        "o_w": inp["L0_ffn_o_w"],
        "ln1_w": inp["L0_ffn_ln1_w"],
        "ln2_w": inp["L0_ffn_ln2_w"],
        "sh_gate": inp["L0_ffn_sh_gate"],
        "sh_up": inp["L0_ffn_sh_up"],
        "sh_down": inp["L0_ffn_sh_down"],
        "r_gate": inp["L0_ffn_r_gate"],
        "r_up": inp["L0_ffn_r_up"],
        "r_down": inp["L0_ffn_r_down"],
        "router_w": inp["L0_ffn_router_w"],
        "router_b": inp["L0_ffn_router_b"],
        "ones_colr": np.ones((128, 1), np.float32),
        "ones_row": np.ones((1, 128), np.float32),
        "one_1": np.ones((1, 1), np.float32),
        "ones_bf": np.ones((128, 128), BF),
    }
    shared = {k: np.ascontiguousarray(v, dtype=(
        BF if k == "ones_bf" else np.float32)) for k, v in shared.items()}

    inv_freq = 1.0 / 10000.0 ** (np.arange(0, RD, 2, dtype=np.float64) / RD)

    def rope_tabs(pos, scale):
        ang = np.outer(pos.astype(np.float64), inv_freq)
        emb = np.concatenate([ang, ang], axis=-1)
        cos = np.cos(emb).T.astype(np.float32)
        sin = np.sin(emb).T.astype(np.float32)
        sinS = sin.copy()
        sinS[:32] *= -1.0
        c2 = np.ascontiguousarray(np.tile(cos * scale, (2, 1)))
        s2 = np.ascontiguousarray(np.tile(sinS * scale, (2, 1)))
        return c2, s2

    in_maps = []
    kidx = np.arange(T)[:, None]
    for c in range(8):
        b, hf = c // 2, c % 2
        m = dict(shared)
        if hf == 0:
            xp = np.concatenate(
                [np.zeros((TOFF, D), np.float32), x[b, 0:TOFF]], 0)
            pos = np.concatenate(
                [np.zeros(TOFF), np.arange(TOFF)]).astype(np.float64)
            gaterow = np.concatenate(
                [np.zeros((1, TOFF)), np.ones((1, TOFF))], 1)
            qpad = (TOFF + np.arange(TOFF))[None, :]
            maskv = ((kidx >= TOFF) & (kidx <= qpad))
        else:
            xp = x[b]
            pos = np.arange(T).astype(np.float64)
            gaterow = np.ones((1, T))
            qpad = (TOFF + np.arange(TOFF))[None, :]
            maskv = (kidx <= qpad)
        m["xT"] = np.ascontiguousarray(xp.T)
        m["gate_row"] = np.ascontiguousarray(gaterow, np.float32)
        m["ktb_col"] = np.full((128, 1), -1e30 if hf == 0 else 0.0,
                               np.float32)
        m["mask_t"] = np.ascontiguousarray(maskv.astype(BF))
        ck, sk = rope_tabs(pos, 1.0)
        cq, sq = rope_tabs(pos, QSCALE)
        m["cosk_t"], m["sinkS_t"] = ck, sk
        m["cosq_t"], m["sinqS_t"] = cq, sq
        in_maps.append(m)
    return in_maps


def kernel(**inputs):
    inp = {k: np.asarray(v) for k, v in inputs.items()}
    nc = build_program()
    in_maps = make_inputs(inp)
    res = run_bass_kernel_spmd(nc, in_maps, core_ids=list(range(8)))
    y = np.empty((B_, S_, D), np.float32)
    for c in range(8):
        b, hf = c // 2, c % 2
        y[b, hf * TOFF:(hf + 1) * TOFF, :] = \
            np.asarray(res.results[c]["out"]).T
    return y


# version 15 (final)
# speedup vs baseline: 1.0711x; 1.0711x over previous
import numpy as np
import ml_dtypes
import concourse.bass as bass
import concourse.mybir as mybir
from concourse.bass_utils import run_bass_kernel_spmd
from concourse.alu_op_type import AluOpType as ALU

F32 = mybir.dt.float32
F32R = mybir.dt.float32r
BF16 = mybir.dt.bfloat16
AF = mybir.ActivationFunctionType
AX = mybir.AxisListType
BF = ml_dtypes.bfloat16

B_, S_, D = 4, 2048, 2048
T = 2048
TH = 512
NTH = T // TH
TOFF = 1024
NCH = D // 128
G = 8
DH = 1024
NH, HD, RD, LAT = 16, 128, 64, 512
LCH = LAT // 128
I_ = 1024
ICH = I_ // 128
NR = 7
QSCALE = float(1.0 / np.sqrt(HD))
TWO_PI = float(2.0 * np.pi)
INV_2PI = float(1.0 / (2.0 * np.pi))
RND_C = float(3 * 2 ** 22)
HALF_PI = float(np.pi / 2)
PI_LO = float(np.nextafter(np.float32(np.pi), np.float32(0.0)))
DEBUG = False
PHASE = "ALL"


class Sem:
    def __init__(self, nc, name):
        self.h = nc.alloc_semaphore(name)
        self.n = 0


class Builder:
    ENGINES = ("sync", "vector", "scalar", "tensor", "gpsimd")

    def __init__(self, nc):
        self.nc = nc
        self.prog = {e: [] for e in self.ENGINES}
        self.eng_sem = {}

    def sem(self, name):
        return Sem(self.nc, name)

    def emit(self, eng, fn, inc=None, amt=1):
        auto = self.eng_sem.get(eng)
        if auto is not None:
            assert inc is None or inc is auto, f"{eng} op must inc its own sem"
            self.wait(eng, auto)
            inc = auto
        if inc is None:
            self.prog[eng].append(fn)
            return None
        inc.n += amt
        self.prog[eng].append(
            lambda e, fn=fn, s=inc.h, a=amt: fn(e).then_inc(s, a))
        return inc.n

    def dma(self, eng, out, in_, sem):
        sem.n += 16
        self.prog[eng].append(
            lambda e, o=out, i=in_, s=sem.h: e.dma_start(out=o, in_=i)
            .then_inc(s, 16))
        return sem.n

    def wait(self, eng, sem, val=None):
        v = sem.n if val is None else val
        assert v <= sem.n, "wait on future event"
        self.prog[eng].append(lambda e, s=sem.h, v=v: e.wait_ge(s, v))

    def build(self):
        nc = self.nc
        with nc.Block() as block:
            @block.sync
            def _(e):
                for f in self.prog["sync"]:
                    f(e)

            @block.vector
            def _(e):
                for f in self.prog["vector"]:
                    f(e)

            @block.scalar
            def _(e):
                for f in self.prog["scalar"]:
                    f(e)

            @block.tensor
            def _(e):
                for f in self.prog["tensor"]:
                    f(e)

            @block.gpsimd
            def _(e):
                for f in self.prog["gpsimd"]:
                    f(e)
        return nc


def build_program(phase=PHASE, debug=DEBUG):
    nc = bass.Bass("TRN2", target_bir_lowering=False)
    B = Builder(nc)

    def din(name, shape, dtype=F32R):
        return nc.dram_tensor(name, list(shape), dtype,
                              kind="ExternalInput").ap()

    xT = din("xT", [D, T])
    W_in = din("W_in", [D, D], F32)
    Wg_x = din("Wg_x", [G, 256, 128], F32)
    Wg_a = din("Wg_a", [G, 256, 128], F32)
    W_out = din("W_out", [D, D], F32)
    b_in = din("b_in", [D], F32)
    bg_x = din("bg_x", [DH], F32)
    bg_a = din("bg_a", [DH], F32)
    b_out = din("b_out", [D], F32)
    a_real = din("a_real", [DH], F32)
    a_imag = din("a_imag", [DH], F32)
    kv_d_w = din("kv_d_w", [D, LAT], F32)
    q_d_w = din("q_d_w", [D, LAT], F32)
    k_u_w = din("k_u_w", [LAT, DH], F32)
    v_u_w = din("v_u_w", [LAT, D], F32)
    q_u_w = din("q_u_w", [LAT, DH], F32)
    rope_k_w = din("rope_k_w", [D, DH], F32)
    rope_q_w = din("rope_q_w", [LAT, DH], F32)
    o_w = din("o_w", [D, D], F32)
    ln1_w = din("ln1_w", [D], F32)
    ln2_w = din("ln2_w", [D], F32)
    sh_gate = din("sh_gate", [D, I_], F32)
    sh_up = din("sh_up", [D, I_], F32)
    sh_down = din("sh_down", [I_, D], F32)
    r_gate = din("r_gate", [NR, D, I_], F32)
    r_up = din("r_up", [NR, D, I_], F32)
    r_down = din("r_down", [NR, I_, D], F32)
    router_w = din("router_w", [D, NR], F32)
    router_b = din("router_b", [NR], F32)
    cosk_t = din("cosk_t", [128, T], F32)
    sinkS_t = din("sinkS_t", [128, T], F32)
    cosq_t = din("cosq_t", [128, T], F32)
    sinqS_t = din("sinqS_t", [128, T], F32)
    mask_t = din("mask_t", [T, TOFF], BF16)
    gate_row = din("gate_row", [1, T], F32)
    ktb_col = din("ktb_col", [128, 1], F32)
    ones_colr = din("ones_colr", [128, 1])
    sel_buf = din("sel_buf", [128, 191])
    ones_row = din("ones_row", [1, 128], F32)
    one_1 = din("one_1", [1, 1], F32)
    ones_bf = din("ones_bf", [128, 128], BF16)

    out_d = nc.dram_tensor("out", [D, TOFF], F32, kind="ExternalOutput").ap()
    dbg = {}
    if debug:
        dbg["x1"] = nc.dram_tensor("d_x1", [D, T], F32R,
                                   kind="ExternalOutput").ap()
        dbg["h"] = nc.dram_tensor("d_h", [D, TH], F32R,
                                  kind="ExternalOutput").ap()
        dbg["h1"] = nc.dram_tensor("d_h1", [D, TOFF], F32,
                                   kind="ExternalOutput").ap()
        dbg["w"] = nc.dram_tensor("d_w", [128, 8 * NR], F32,
                                  kind="ExternalOutput").ap()

    g_bf = nc.dram_tensor("g_bf", [8, D, I_], BF16).ap()
    u_bf = nc.dram_tensor("u_bf", [8, D, I_], BF16).ap()
    d_bf = nc.dram_tensor("d_bf", [8, I_, D], BF16).ap()
    wgx_bf = nc.dram_tensor("wgx_bf", [G, 256, 128], BF16).ap()
    wga_bf = nc.dram_tensor("wga_bf", [G, 256, 128], BF16).ap()
    win_bf = nc.dram_tensor("win_bf", [D, D], BF16).ap()
    wout_bf = nc.dram_tensor("wout_bf", [D, D], BF16).ap()
    kvd_bf = nc.dram_tensor("kvd_bf", [D, LAT], BF16).ap()
    qd_bf = nc.dram_tensor("qd_bf", [D, LAT], BF16).ap()
    ku_bf = nc.dram_tensor("ku_bf", [LAT, DH], BF16).ap()
    vu_bf = nc.dram_tensor("vu_bf", [LAT, D], BF16).ap()
    qu_bf = nc.dram_tensor("qu_bf", [LAT, DH], BF16).ap()
    ropek_bf = nc.dram_tensor("ropek_bf", [D, DH], BF16).ap()
    ropeq_bf = nc.dram_tensor("ropeq_bf", [LAT, DH], BF16).ap()
    ow_bf = nc.dram_tensor("ow_bf", [D, D], BF16).ap()
    x1_d = nc.dram_tensor("x1_d", [D, T], F32R).ap()
    xo_d = nc.dram_tensor("xo_d", [D, TOFF], F32).ap()
    h1_d = nc.dram_tensor("h1_d", [D, TOFF], F32).ap()
    k_d = nc.dram_tensor("k_d", [NH * 128, T], BF16).ap()
    v_d = nc.dram_tensor("v_d", [NH * T, HD], BF16).ap()
    q_dram = nc.dram_tensor("q_dram", [NH * 128, TOFF], BF16).ap()

    sb = nc.alloc_sbuf_tensor
    SA = sb("SA", [128, NCH * TH], F32R).ap()
    SB = sb("SB", [128, NCH * TH], F32).ap()
    SD = sb("SD", [128, NCH * TH], BF16).ap()
    SE = sb("SE", [128, NCH * TH], BF16).ap()
    SF = sb("SF", [128, NCH * 1024], BF16).ap()
    NSL = 10
    SPV = sb("SPV", [128, NSL * TH], F32).ap()
    SQT = sb("SQT", [128, 2 * TH], F32R).ap()
    WBUFB = sb("WBUFB", [128, 4 * 512], BF16).ap()
    QKT = SE
    SBR = SB.bitcast(F32R)
    CONST = sb("CONST", [128, 152], F32).ap()
    RW = sb("RW", [128, NCH * NR], F32).ap()
    WL2 = sb("WL2", [128, 2 * NR], F32).ap()
    RB = sb("RB", [128, NR], F32).ap()
    WTOK = sb("WTOK", [128, 8 * NR], F32).ap()
    CARRY = sb("CARRY", [128, 3 * G], F32).ap()
    GTT = sb("GTT", [128, TH], F32).ap()
    GT = GTT
    MROW = sb("MROW", [1, 3 * NH], F32).ap()
    ONECR = sb("ONECR", [128, 1], F32R).ap()
    SELB = sb("SELB", [128, 191], F32R).ap()
    ONER = sb("ONER", [1, 128], F32).ap()
    ONE1 = sb("ONE1", [1, 1], F32).ap()
    ONEBF = sb("ONEBF", [128, 128], BF16).ap()

    ps = nc.alloc_psum_tensor
    PA = ps("PA", [128, 1024], F32).ap()
    PB = ps("PB", [128, 1024], F32).ap()
    PC = ps("PC", [128, 1024], F32).ap()
    PD = ps("PD", [128, 1024], F32).ap()

    C_BIN, C_BOUT = 0, 16
    C_BGX, C_BGA, C_SP8, C_AI = 32, 40, 48, 56
    C_LN1, C_LN2 = 64, 80
    C_MH = 96
    C_T0, C_T1 = 112, 113
    C_EPS, C_ONE = 114, 115
    C_MHX = 116
    C_KTB = 148

    def ccol(i):
        return CONST[:, i:i + 1]

    s_lw = B.sem("s_lw")
    s_ls = B.sem("s_ls")
    s_lg = B.sem("s_lg")
    s_x = B.sem("s_x")
    s_gr = B.sem("s_gr")
    s_st = B.sem("s_st")
    s_sl = [B.sem("s_sl%d" % i) for i in range(4)]
    s_kv2 = [B.sem("s_kv0"), B.sem("s_kv1")]
    s_qt2 = [B.sem("s_qt0"), B.sem("s_qt1")]
    s_cast = B.sem("s_cast")
    s_t = B.sem("s_t")
    s_v = B.sem("s_v")
    s_s = B.sem("s_s")
    B.eng_sem = {"vector": s_v, "scalar": s_s}

    def barrier(engines=Builder.ENGINES):
        for eng in engines:
            for s in (s_lw, s_ls, s_lg, s_x, s_gr, s_st, s_t, s_v, s_s,
                      s_sl[0], s_sl[1], s_sl[2], s_sl[3],
                      s_kv2[0], s_kv2[1], s_qt2[0], s_qt2[1]):
                B.wait(eng, s)

    CSEM = {k: B.sem("s_cast_" + k) for k in
            ("win", "wg", "wout", "proj", "ow", "moe")}
    CAST = {}
    B.dma("gpsimd", win_bf, W_in, CSEM["win"])
    CAST["win"] = ("win", CSEM["win"].n)
    for g in range(G):
        B.dma("gpsimd", wgx_bf[g], Wg_x[g], CSEM["wg"])
        B.dma("gpsimd", wga_bf[g], Wg_a[g], CSEM["wg"])
    CAST["wg"] = ("wg", CSEM["wg"].n)
    B.dma("gpsimd", wout_bf, W_out, CSEM["wout"])
    CAST["wout"] = ("wout", CSEM["wout"].n)
    B.dma("gpsimd", kvd_bf, kv_d_w, CSEM["proj"])
    B.dma("gpsimd", qd_bf, q_d_w, CSEM["proj"])
    B.dma("gpsimd", ku_bf, k_u_w, CSEM["proj"])
    B.dma("gpsimd", vu_bf, v_u_w, CSEM["proj"])
    B.dma("gpsimd", qu_bf, q_u_w, CSEM["proj"])
    B.dma("gpsimd", ropek_bf, rope_k_w, CSEM["proj"])
    B.dma("gpsimd", ropeq_bf, rope_q_w, CSEM["proj"])
    CAST["proj"] = ("proj", CSEM["proj"].n)

    for k in range(NCH):
        B.dma("sync", ccol(C_BIN + k), b_in[k * 128:(k + 1) * 128].unsqueeze(1),
              s_lw)
        B.dma("sync", ccol(C_BOUT + k),
              b_out[k * 128:(k + 1) * 128].unsqueeze(1), s_lw)
        B.dma("sync", ccol(C_LN1 + k),
              ln1_w[k * 128:(k + 1) * 128].unsqueeze(1), s_lw)
        B.dma("sync", ccol(C_LN2 + k),
              ln2_w[k * 128:(k + 1) * 128].unsqueeze(1), s_lw)
        B.dma("sync", RW[:, k * NR:(k + 1) * NR],
              router_w[k * 128:(k + 1) * 128, :], s_lw)
    for g in range(G):
        B.dma("sync", ccol(C_BGX + g), bg_x[g * 128:(g + 1) * 128].unsqueeze(1),
              s_lw)
        B.dma("sync", ccol(C_BGA + g), bg_a[g * 128:(g + 1) * 128].unsqueeze(1),
              s_lw)
        B.dma("sync", ccol(C_SP8 + g),
              a_real[g * 128:(g + 1) * 128].unsqueeze(1), s_lw)
        B.dma("sync", ccol(C_AI + g),
              a_imag[g * 128:(g + 1) * 128].unsqueeze(1), s_lw)
    B.dma("sync", ONECR, ones_colr, s_lw)
    B.dma("sync", SELB, sel_buf, s_lw)
    B.dma("sync", ONER, ones_row, s_lw)
    B.dma("sync", ONE1, one_1, s_lw)
    B.dma("sync", ONEBF, ones_bf, s_lw)
    B.dma("sync", ccol(C_KTB), ktb_col, s_lw)
    B.dma("sync", RB, router_b.unsqueeze(0).broadcast_to([128, NR]), s_lw)
    barrier()

    B.emit("vector", lambda e: e.memset(MROW, 0.0), inc=s_v)
    B.emit("vector", lambda e: e.memset(ccol(C_EPS), 1e-5))
    B.emit("vector", lambda e: e.memset(ccol(C_ONE), 1.0), inc=s_v)
    B.wait("scalar", s_v)

    t0, t1 = ccol(C_T0), ccol(C_T1)
    for g in range(G):
        a_g = ccol(C_SP8 + g)
        B.emit("scalar", lambda e, a=a_g: e.activation(t0, a, AF.Abs))
        B.emit("scalar", lambda e: e.activation(t0, t0, AF.Exp, scale=-1.0))
        B.emit("scalar", lambda e: e.activation(t0, t0, AF.Ln, bias=ccol(C_ONE)),
               inc=s_s)
        B.wait("vector", s_s)
        B.emit("vector", lambda e, a=a_g: e.tensor_scalar_max(t1, a, 0.0),
               inc=s_v)
        B.wait("vector", s_v)
        B.emit("vector", lambda e: e.tensor_add(t1, t1, t0), inc=s_v)
        B.wait("vector", s_v)
        B.emit("vector", lambda e, a=a_g: e.tensor_scalar_mul(a, t1, -8.0),
               inc=s_v)
        B.wait("scalar", s_v)
    guard = {0: None, 1: None}
    bslab_guard = [None, None, None, None]

    def gemm(W, KCH, MOUT, rhs_fn, evac, rhs_ready=None, n_free=TH,
             cast_ready=None):
        MB = 512
        nmb = (MOUT + MB - 1) // MB
        if rhs_ready is not None:
            B.wait("tensor", rhs_ready[0], rhs_ready[1])
        if cast_ready is not None:
            B.wait("sync", CSEM[cast_ready[0]], cast_ready[1])
        for mb in range(nmb):
            m0 = mb * MB
            mbw = min(MB, MOUT - m0)
            nmc = (mbw + 127) // 128
            pbank = (PA, PB) if mb % 2 == 0 else (PC, PD)
            if guard[mb % 2] is not None:
                B.wait("tensor", *guard[mb % 2])
                guard[mb % 2] = None
            for k in range(KCH):
                si = (mb * KCH + k) % 4
                sl_ = WBUFB[:, si * 512:si * 512 + mbw]
                if bslab_guard[si] is not None:
                    B.wait("sync", *bslab_guard[si])
                wv = B.dma("sync", sl_, W[k * 128:(k + 1) * 128, m0:m0 + mbw],
                           s_sl[si])
                B.wait("tensor", s_sl[si], wv)
                tv = None
                for mc in range(nmc):
                    mw = min(128, mbw - mc * 128)
                    pt = pbank[mc // 2][:, (mc % 2) * 512:(mc % 2) * 512 + n_free]
                    tv = B.emit(
                        "tensor",
                        lambda e, p=pt[0:mw, :], w=sl_[:, mc * 128:mc * 128 + mw],
                        r=rhs_fn(k), st=(k == 0), sp=(k == KCH - 1):
                        e.matmul(p, w, r, start=st, stop=sp),
                        inc=(s_t if mc == nmc - 1 else None))
                bslab_guard[si] = (s_t, tv)
            mm_done = s_t.n
            last = None
            for mc in range(nmc):
                mw = min(128, mbw - mc * 128)
                pt = pbank[mc // 2][:, (mc % 2) * 512:(mc % 2) * 512 + n_free]
                last = evac(m0 + mc * 128, mw, pt[0:mw, :], mm_done)
            guard[mb % 2] = last

    def norm(src_fn, kind, out_fn=None, extra=None):
        if guard[1] is not None:
            B.wait("tensor", *guard[1])
            guard[1] = None
        sum_r = PC[0:1, 0:TH]
        sq_r = PC[0:1, 512:512 + TH]
        sqv = {}
        for k in range(NCH):
            xt = src_fn(k)
            sqt = SQT[:, 0:TH]
            if k >= 1:
                B.wait("vector", s_t, sqv[k - 1])
            B.emit("vector", lambda e, x=xt, q=sqt: e.tensor_mul(q, x, x),
                   inc=s_v)
            B.wait("tensor", s_v)
            B.emit("tensor", lambda e, x=xt, st=(k == 0), sp=(k == NCH - 1):
                   e.matmul(sum_r, ONECR, x, start=st, stop=sp))
            sqv[k] = B.emit("tensor",
                            lambda e, q=sqt, st=(k == 0), sp=(k == NCH - 1):
                            e.matmul(sq_r, ONECR, q, start=st, stop=sp),
                            inc=s_t)
        B.wait("vector", s_t)
        mean = SPV[0:1, 0:TH]
        var = SPV[0:1, TH:2 * TH]
        rstd = SPV[0:1, 2 * TH:3 * TH]
        nmr = SPV[0:1, 3 * TH:4 * TH]
        B.emit("vector", lambda e: e.tensor_scalar_mul(mean, sum_r, 1.0 / D))
        B.emit("vector", lambda e: e.tensor_scalar_mul(var, sq_r, 1.0 / D),
               inc=s_v)
        B.wait("vector", s_v)
        B.emit("vector", lambda e: e.tensor_mul(nmr, mean, mean), inc=s_v)
        B.wait("vector", s_v)
        B.emit("vector", lambda e: e.tensor_sub(var, var, nmr), inc=s_v)
        B.wait("scalar", s_v)
        B.emit("scalar", lambda e: e.activation(rstd, var, AF.Ln, bias=ccol(C_EPS)[0:1, :]))
        B.emit("scalar", lambda e: e.activation(rstd, rstd, AF.Exp,
                                                scale=-0.5), inc=s_s)
        B.wait("vector", s_s)
        rows = dict(mean=mean, var=var, rstd=rstd)
        if extra is not None:
            extra(rows)
        B.emit("vector", lambda e: e.tensor_mul(nmr, mean, rstd), inc=s_v)
        B.wait("vector", s_v)
        B.emit("vector", lambda e: e.tensor_scalar_mul(nmr, nmr, -1.0),
               inc=s_v)
        B.wait("tensor", s_v)
        B.emit("tensor", lambda e: e.matmul(PD[:, 0:TH], ONER, rstd,
                                            start=True, stop=True))
        B.emit("tensor", lambda e: e.matmul(PD[:, 512:512 + TH], ONER, nmr,
                                            start=True, stop=True), inc=s_t)
        B.wait("vector", s_t)
        if out_fn is not None:
            for k in range(NCH):
                xt = src_fn(k)
                ot = out_fn(k)
                B.emit("vector", lambda e, x=xt: e.tensor_mul(
                    SPV[:, 4 * TH:5 * TH], x, PD[:, 0:TH]), inc=s_v)
                B.wait("vector", s_v)
                B.emit("vector", lambda e, o=ot: e.tensor_add(
                    o, SPV[:, 4 * TH:5 * TH], PD[:, 512:512 + TH]), inc=s_v)
        B.emit("vector", lambda e: e.nop(), inc=s_v)
        guard[1] = (s_v, s_v.n)
        return s_v.n

    def stage_b(th):
        tok = slice(th * TH, (th + 1) * TH)
        B.wait("sync", s_v)
        B.wait("sync", s_st)
        for k in range(NCH):
            B.dma("sync", SA[:, k * TH:(k + 1) * TH],
                  xT[k * 128:(k + 1) * 128, tok], s_x)
        XLD = s_x.n
        B.wait("vector", s_x, XLD)
        B.wait("tensor", s_x, XLD)
        ln_ms = norm(lambda k: SA[:, k * TH:(k + 1) * TH], "ln",
                     out_fn=lambda k: SF[:, k * TH:(k + 1) * TH])

        def evac_xh(m0, mw, p, tv):
            k = m0 // 128
            B.wait("vector", s_t, tv)
            v = B.emit("vector", lambda e, p=p, k=k: e.tensor_scalar(
                SB[:, k * TH:(k + 1) * TH], p, ccol(C_BIN + k), None,
                ALU.add), inc=s_v)
            return (s_v, v)

        gemm(win_bf, NCH, D, lambda k: SF[:, k * TH:(k + 1) * TH], evac_xh,
             rhs_ready=(s_v, ln_ms), cast_ready=CAST["win"])
        XH_OK = s_v.n
        B.wait("vector", s_v, XH_OK)
        B.wait("vector", s_lg)
        B.wait("vector", s_ls)
        for k in range(NCH):
            B.emit("vector", lambda e, k=k: e.tensor_copy(
                SE[:, k * 512:(k + 1) * 512], SB[:, k * TH:(k + 1) * TH]))
        B.emit("vector", lambda e: e.nop(), inc=s_v)
        XHB_OK = s_v.n

        if guard[1] is not None:
            B.wait("tensor", *guard[1])
            guard[1] = None
        B.wait("sync", s_v)
        B.wait("sync", s_t)
        gv_ = B.dma("sync", SPV[0:1, 0:TH], gate_row[0:1, tok], s_gr)
        B.wait("tensor", s_gr, gv_)
        B.emit("tensor", lambda e: e.matmul(
            PD[:, 0:TH], ONER, SPV[0:1, 0:TH], start=True, stop=True),
            inc=s_t)
        B.wait("vector", s_t)
        B.emit("vector", lambda e: e.tensor_copy(GT[:, 0:TH], PD[:, 0:TH]),
               inc=s_v)
        B.emit("vector", lambda e: e.nop(), inc=s_v)
        guard[1] = (s_v, s_v.n)
        def sl(i):
            return SPV[:, i * TH:(i + 1) * TH]

        for g in range(G):
            B.wait("tensor", s_v, XHB_OK)
            for wi, (wsrc, bc, dst) in enumerate(
                    ((wgx_bf, C_BGX + g, sl(1)), (wga_bf, C_BGA + g, sl(0)))):
                si = wi % 2
                if bslab_guard[si] is not None:
                    B.wait("sync", *bslab_guard[si])
                B.wait("sync", CSEM[CAST["wg"][0]], CAST["wg"][1])
                B.dma("sync", WBUFB[:, si * 512:si * 512 + 128],
                      wsrc[g, 0:128, :], s_sl[si])
                wv = B.dma("sync", WBUFB[:, si * 512 + 128:si * 512 + 256],
                           wsrc[g, 128:256, :], s_sl[si])
                B.wait("tensor", s_sl[si], wv)
                if guard[0] is not None:
                    B.wait("tensor", *guard[0])
                    guard[0] = None
                tv = None
                for kk in range(2):
                    tv = B.emit("tensor", lambda e, kk=kk, si=si, g=g:
                                e.matmul(PA[:, 0:TH],
                                         WBUFB[:, si * 512 + kk * 128:
                                               si * 512 + (kk + 1) * 128],
                                         SE[:, (2 * g + kk) * 512:
                                            (2 * g + kk + 1) * 512],
                                         start=(kk == 0), stop=(kk == 1)),
                                inc=(s_t if kk == 1 else None))
                bslab_guard[si] = (s_t, tv)
                B.wait("scalar", s_t, tv)
                sv = B.emit("scalar", lambda e, d=dst, bc=bc: e.activation(
                    d, PA[:, 0:TH], AF.Sigmoid, bias=ccol(bc)), inc=s_s)
                guard[0] = (s_s, sv)
            B.wait("vector", s_s)
            B.emit("vector", lambda e, g=g: e.tensor_scalar(
                sl(5), sl(0), ccol(C_SP8 + g), None, ALU.mult))
            B.emit("vector", lambda e, g=g: e.tensor_scalar(
                sl(2), sl(0), ccol(C_AI + g), None, ALU.mult), inc=s_v)
            B.wait("scalar", s_v)
            B.emit("scalar", lambda e: e.activation(sl(3), sl(5), AF.Exp),
                   inc=s_s)
            B.wait("vector", s_s)
            B.emit("vector", lambda e: e.tensor_mul(sl(8), sl(3), sl(3)),
                   inc=s_v)
            B.wait("vector", s_v)
            B.emit("vector", lambda e: e.tensor_scalar(
                sl(8), sl(8), -1.0, 1.0, ALU.mult, ALU.add), inc=s_v)
            B.wait("scalar", s_v)
            B.emit("scalar", lambda e: e.activation(sl(8), sl(8), AF.Sqrt),
                   inc=s_s)
            B.wait("vector", s_s)
            B.emit("vector", lambda e: e.tensor_mul(sl(1), sl(1), sl(8)),
                   inc=s_v)
            B.wait("vector", s_v)
            B.emit("vector", lambda e: e.tensor_mul(sl(1), sl(1), GT[:, 0:TH]),
                   inc=s_v)
            B.wait("vector", s_v)
            B.emit("vector", lambda e, g=g: e.tensor_mul(
                sl(5), sl(1), SB[:, g * TH:(g + 1) * TH]))
            B.emit("vector", lambda e, g=g: e.tensor_mul(
                sl(6), sl(1), SB[:, (8 + g) * TH:(9 + g) * TH]), inc=s_v)
            B.wait("vector", s_v)
            phi_init = 0.0 if th == 0 else CARRY[:, 3 * g:3 * g + 1]
            B.emit("vector", lambda e: e.memset(sl(8), 0.0), inc=s_v)
            B.wait("vector", s_v)
            B.emit("vector", lambda e, ini=phi_init: e.tensor_tensor_scan(
                sl(7), sl(2), sl(8), ini, ALU.add, ALU.add), inc=s_v)
            B.wait("vector", s_v)
            B.emit("vector", lambda e: e.tensor_scalar(
                sl(8), sl(7), INV_2PI, RND_C, ALU.mult, ALU.add), inc=s_v)
            B.wait("vector", s_v)
            B.emit("vector", lambda e: e.tensor_scalar(
                sl(8), sl(8), RND_C, None, ALU.subtract), inc=s_v)
            B.wait("vector", s_v)
            B.emit("vector", lambda e: e.scalar_tensor_tensor(
                sl(8), sl(8), -TWO_PI, sl(7), ALU.mult, ALU.add), inc=s_v)
            B.wait("vector", s_v)
            B.emit("vector", lambda e: e.tensor_scalar(
                sl(8), sl(8), PI_LO, -PI_LO, ALU.min, ALU.max), inc=s_v)
            B.wait("scalar", s_v)
            B.emit("scalar", lambda e: e.activation(sl(9), sl(8), AF.Sin),
                   inc=s_s)
            B.wait("vector", s_s)
            B.emit("vector", lambda e: e.tensor_scalar(
                sl(8), sl(7), INV_2PI, 0.25, ALU.mult, ALU.add), inc=s_v)
            B.wait("vector", s_v)
            B.emit("vector", lambda e: e.tensor_scalar(
                sl(8), sl(8), RND_C, RND_C, ALU.add, ALU.subtract), inc=s_v)
            B.wait("vector", s_v)
            B.emit("vector", lambda e: e.scalar_tensor_tensor(
                sl(8), sl(8), -TWO_PI, sl(7), ALU.mult, ALU.add), inc=s_v)
            B.wait("vector", s_v)
            B.emit("vector", lambda e: e.tensor_scalar(
                sl(8), sl(8), HALF_PI, None, ALU.add), inc=s_v)
            B.wait("vector", s_v)
            B.emit("vector", lambda e: e.tensor_scalar(
                sl(8), sl(8), PI_LO, -PI_LO, ALU.min, ALU.max), inc=s_v)
            B.wait("scalar", s_v)
            B.emit("scalar", lambda e: e.activation(sl(4), sl(8), AF.Sin),
                   inc=s_s)
            B.wait("vector", s_s)
            B.emit("vector", lambda e: e.tensor_mul(sl(8), sl(4), sl(5)))
            B.emit("vector", lambda e: e.tensor_mul(sl(2), sl(9), sl(6)),
                   inc=s_v)
            B.wait("vector", s_v)
            B.emit("vector", lambda e: e.tensor_add(sl(8), sl(8), sl(2)))
            B.emit("vector", lambda e: e.tensor_mul(sl(2), sl(4), sl(6)))
            B.emit("vector", lambda e: e.tensor_mul(sl(5), sl(9), sl(5)),
                   inc=s_v)
            B.wait("vector", s_v)
            B.emit("vector", lambda e: e.tensor_sub(sl(2), sl(2), sl(5)),
                   inc=s_v)
            B.wait("vector", s_v)
            gr_init = 0.0 if th == 0 else CARRY[:, 3 * g + 1:3 * g + 2]
            gi_init = 0.0 if th == 0 else CARRY[:, 3 * g + 2:3 * g + 3]
            B.emit("vector", lambda e, ini=gr_init: e.tensor_tensor_scan(
                sl(5), sl(3), sl(8), ini, ALU.mult, ALU.add))
            B.emit("vector", lambda e, ini=gi_init: e.tensor_tensor_scan(
                sl(6), sl(3), sl(2), ini, ALU.mult, ALU.add), inc=s_v)
            B.wait("vector", s_v)
            B.emit("vector", lambda e, g=g: e.tensor_copy(
                CARRY[:, 3 * g:3 * g + 1], sl(7)[:, TH - 1:TH]))
            B.emit("vector", lambda e, g=g: e.tensor_copy(
                CARRY[:, 3 * g + 1:3 * g + 2], sl(5)[:, TH - 1:TH]))
            B.emit("vector", lambda e, g=g: e.tensor_copy(
                CARRY[:, 3 * g + 2:3 * g + 3], sl(6)[:, TH - 1:TH]), inc=s_v)
            B.wait("vector", s_v)
            B.emit("vector", lambda e: e.tensor_mul(sl(8), sl(4), sl(5)))
            B.emit("vector", lambda e: e.tensor_mul(sl(2), sl(9), sl(6)),
                   inc=s_v)
            B.wait("vector", s_v)
            B.emit("vector", lambda e, g=g: e.tensor_sub(
                SE[:, (2 * g) * TH:(2 * g + 1) * TH], sl(8), sl(2)))
            B.emit("vector", lambda e: e.tensor_mul(sl(8), sl(9), sl(5)))
            B.emit("vector", lambda e: e.tensor_mul(sl(2), sl(4), sl(6)),
                   inc=s_v)
            B.wait("vector", s_v)
            B.emit("vector", lambda e, g=g: e.tensor_add(
                SE[:, (2 * g + 1) * TH:(2 * g + 2) * TH], sl(8), sl(2)),
                inc=s_v)
            B.wait("vector", s_v)
        H_OK = s_v.n

        def evac_y(m0, mw, p, tv):
            k = m0 // 128
            B.wait("vector", s_t, tv)
            v = B.emit("vector", lambda e, p=p, k=k: e.scalar_tensor_tensor(
                SA[:, k * TH:(k + 1) * TH], p, ccol(C_BOUT + k),
                SA[:, k * TH:(k + 1) * TH], ALU.add, ALU.add), inc=s_v)
            return (s_v, v)

        gemm(wout_bf, NCH, D, lambda k: SE[:, k * TH:(k + 1) * TH], evac_y,
             rhs_ready=(s_v, H_OK), cast_ready=CAST["wout"])
        X1_OK = s_v.n
        B.wait("sync", s_v, X1_OK)
        for k in range(NCH):
            B.dma("sync", x1_d[k * 128:(k + 1) * 128, tok],
                  SA[:, k * TH:(k + 1) * TH], s_st)
            if debug:
                B.dma("sync", dbg["x1"][k * 128:(k + 1) * 128, tok],
                      SA[:, k * TH:(k + 1) * TH], s_st)

        def extra_rows(rows):
            var = rows["var"]
            rstd = rows["rstd"]
            r2 = SPV[0:1, 5 * TH:6 * TH]
            B.emit("vector", lambda e: e.tensor_mul(r2, rstd, rstd), inc=s_v)
            B.wait("vector", s_v)
            B.emit("vector", lambda e: e.tensor_mul(r2, r2, var), inc=s_v)
            B.wait("scalar", s_v)
            B.emit("scalar", lambda e: e.activation(r2, r2, AF.Ln, bias=ccol(C_EPS)[0:1, :]))
            B.emit("scalar", lambda e: e.activation(r2, r2, AF.Exp,
                                                    scale=-0.5), inc=s_s)
            B.wait("vector", s_s)

        ln2_ms = norm(lambda k: SA[:, k * TH:(k + 1) * TH], "ln",
                      out_fn=lambda k: SB[:, k * TH:(k + 1) * TH],
                      extra=extra_rows)
        if th >= 2:
            B.wait("sync", s_v, ln2_ms)
            otok = slice((th - 2) * TH, (th - 1) * TH)
            for k in range(NCH):
                B.dma("sync", xo_d[k * 128:(k + 1) * 128, otok],
                      SB[:, k * TH:(k + 1) * TH], s_st)
        if guard[1] is not None:
            B.wait("tensor", *guard[1])
            guard[1] = None
        B.wait("tensor", s_v, ln2_ms)
        B.emit("tensor", lambda e: e.matmul(
            PD[:, 0:TH], ONER, SPV[0:1, 5 * TH:6 * TH], start=True,
            stop=True), inc=s_t)
        B.wait("vector", s_t)
        for k in range(NCH):
            B.emit("vector", lambda e, k=k: e.scalar_tensor_tensor(
                SF[:, k * TH:(k + 1) * TH], SB[:, k * TH:(k + 1) * TH],
                ccol(C_LN1 + k), PD[:, 0:TH], ALU.mult, ALU.mult), inc=s_v)
        RMS1_OK = s_v.n
        B.emit("vector", lambda e: e.nop(), inc=s_v)
        guard[1] = (s_v, s_v.n)

        def copy_evac(base):
            def evac(m0, mw, p, tv):
                B.wait("vector", s_t, tv)
                v = B.emit("vector", lambda e, p=p, d=SD[
                    :, (base + m0 // 128) * TH:(base + m0 // 128 + 1) * TH]:
                    e.tensor_copy(d, p), inc=s_v)
                return (s_v, v)
            return evac

        gemm(kvd_bf, NCH, LAT, lambda k: SF[:, k * TH:(k + 1) * TH],
             copy_evac(0), rhs_ready=(s_v, RMS1_OK), cast_ready=CAST["proj"])
        if th >= 2:
            gemm(qd_bf, NCH, LAT, lambda k: SF[:, k * TH:(k + 1) * TH],
                 copy_evac(4))
        LAT_OK = s_v.n

        def stage_k_evac(colbase):
            def evac(m0, mw, p, tv):
                hp = m0 // 128
                par = hp % 2
                tile = QKT[:, (2 + par) * 512:(3 + par) * 512]
                B.wait("vector", s_t, tv)
                B.wait("vector", s_lg)
                v = B.emit("vector", lambda e, p=p, t=tile: e.tensor_copy(
                    t, p), inc=s_v)
                B.wait("gpsimd", s_v, v)
                B.dma("gpsimd", k_d[(2 * hp) * 128 + colbase:
                                    (2 * hp) * 128 + colbase + 64, tok],
                      tile[0:64, :], s_lg)
                B.dma("gpsimd", k_d[(2 * hp + 1) * 128 + colbase:
                                    (2 * hp + 1) * 128 + colbase + 64, tok],
                      tile[64:128, :], s_lg)
                return (s_v, v)
            return evac

        gemm(ku_bf, LCH, DH, lambda k: SD[:, k * TH:(k + 1) * TH],
             stage_k_evac(0), rhs_ready=(s_v, LAT_OK))

        if th >= 2:
            otok = slice((th - 2) * TH, (th - 1) * TH)

            def stage_q_evac(colbase):
                def evac(m0, mw, p, tv):
                    hp = m0 // 128
                    par = hp % 2
                    tile = QKT[:, (2 + par) * 512:(3 + par) * 512]
                    B.wait("vector", s_t, tv)
                    B.wait("vector", s_lg)
                    v = B.emit("vector", lambda e, p=p, t=tile:
                               e.tensor_scalar_mul(t, p, QSCALE), inc=s_v)
                    B.wait("gpsimd", s_v, v)
                    B.dma("gpsimd", q_dram[(2 * hp) * 128 + colbase:
                                           (2 * hp) * 128 + colbase + 64,
                                           otok], tile[0:64, :], s_lg)
                    B.dma("gpsimd", q_dram[(2 * hp + 1) * 128 + colbase:
                                           (2 * hp + 1) * 128 + colbase + 64,
                                           otok], tile[64:128, :], s_lg)
                    return (s_v, v)
                return evac

            gemm(qu_bf, LCH, DH, lambda k: SD[:, (4 + k) * TH:(5 + k) * TH],
                 stage_q_evac(0))

        B.wait("tensor", s_v, LAT_OK)
        for vb in range(4):
            pbank = (PA, PB) if vb % 2 == 0 else (PC, PD)
            if guard[vb % 2] is not None:
                B.wait("tensor", *guard[vb % 2])
                guard[vb % 2] = None
            for lat in range(LCH):
                si = (vb * LCH + lat) % 4
                if bslab_guard[si] is not None:
                    B.wait("sync", *bslab_guard[si])
                wv = B.dma("sync", WBUFB[:, si * 512:(si + 1) * 512],
                           vu_bf[lat * 128:(lat + 1) * 128,
                                 vb * 512:(vb + 1) * 512], s_sl[si])
                B.wait("tensor", s_sl[si], wv)
                tvv = None
                for tc in range(4):
                    pt = pbank[tc // 2][:, (tc % 2) * 512:(tc % 2) * 512 + 512]
                    tvv = B.emit("tensor", lambda e, p=pt,
                                 w=SD[:, lat * TH + tc * 128:
                                      lat * TH + tc * 128 + 128],
                                 r=WBUFB[:, si * 512:(si + 1) * 512],
                                 st=(lat == 0), sp=(lat == LCH - 1):
                                 e.matmul(p, w, r, start=st, stop=sp),
                                 inc=(s_t if tc == 3 else None))
                bslab_guard[si] = (s_t, tvv)
            VMM = s_t.n
            last = None
            for tc in range(4):
                pt = pbank[tc // 2][:, (tc % 2) * 512:(tc % 2) * 512 + 512]
                par = tc % 2
                tile = QKT[:, (2 + par) * 512:(3 + par) * 512]
                B.wait("vector", s_t, VMM)
                B.wait("vector", s_lg)
                v = B.emit("vector", lambda e, p=pt, t=tile: e.tensor_copy(
                    t, p), inc=s_v)
                B.wait("gpsimd", s_v, v)
                for j in range(4):
                    h = vb * 4 + j
                    B.dma("gpsimd",
                          v_d[h * T + th * TH + tc * 128:
                              h * T + th * TH + tc * 128 + 128, :],
                          tile[:, j * 128:(j + 1) * 128], s_lg)
                last = (s_v, v)
            guard[vb % 2] = last

        def load_rope_tabs(ctab, stab):
            B.wait("scalar", s_v)
            B.dma("scalar", SPV[:, 8 * TH:9 * TH], ctab[:, tok], s_ls)
            B.dma("scalar", SPV[:, 9 * TH:10 * TH], stab[:, tok], s_ls)
            return s_ls.n

        def mk_rope_evac(dst_rows, toksl):
            def evac(m0, mw, p, tv):
                hp = m0 // 128
                rt = SPV[:, 6 * TH:7 * TH]
                sh = SPV[:, 7 * TH:8 * TH]
                B.wait("vector", s_t, tv)
                B.wait("vector", s_lg)
                v0 = B.emit("vector", lambda e, p=p: e.tensor_copy(rt, p),
                            inc=s_v)
                B.wait("gpsimd", s_v, v0)
                B.dma("gpsimd", sh[0:32, :], rt[32:64, :], s_lg)
                B.dma("gpsimd", sh[32:64, :], rt[0:32, :], s_lg)
                B.dma("gpsimd", sh[64:96, :], rt[96:128, :], s_lg)
                B.dma("gpsimd", sh[96:128, :], rt[64:96, :], s_lg)
                B.wait("vector", s_lg)
                B.emit("vector", lambda e: e.tensor_mul(
                    rt, rt, SPV[:, 8 * TH:9 * TH]))
                B.emit("vector", lambda e: e.tensor_mul(
                    sh, sh, SPV[:, 9 * TH:10 * TH]), inc=s_v)
                B.wait("vector", s_v)
                tile = QKT[:, 2 * 512:3 * 512]
                B.wait("vector", s_ls)
                v = B.emit("vector", lambda e: e.tensor_add(tile, rt, sh),
                           inc=s_v)
                B.wait("gpsimd", s_v, v)
                B.dma("gpsimd", dst_rows((2 * hp) * 128 + 64,
                                         (2 * hp) * 128 + 128, toksl),
                      tile[0:64, :], s_lg)
                B.dma("gpsimd", dst_rows((2 * hp + 1) * 128 + 64,
                                         (2 * hp + 1) * 128 + 128, toksl),
                      tile[64:128, :], s_lg)
                B.wait("tensor", s_v, v)
                return (s_v, v)
            return evac

        load_rope_tabs(cosk_t, sinkS_t)
        B.wait("vector", s_ls)
        gemm(ropek_bf, NCH, DH, lambda k: SF[:, k * TH:(k + 1) * TH],
             mk_rope_evac(lambda a, b, ts: k_d[a:b, ts], tok))
        if th >= 2:
            load_rope_tabs(cosq_t, sinqS_t)
            B.wait("vector", s_ls)
            gemm(ropeq_bf, LCH, DH,
                 lambda k: SD[:, (4 + k) * TH:(5 + k) * TH],
                 mk_rope_evac(lambda a, b, ts: q_dram[a:b, ts], otok))


    for th in range(NTH):
        stage_b(th)

    B.dma("gpsimd", ow_bf, o_w, CSEM["ow"])
    CAST["ow"] = ("ow", CSEM["ow"].n)
    for e in range(8):
        B.dma("gpsimd", g_bf[e], sh_gate if e == 0 else r_gate[e - 1],
              CSEM["moe"])
        B.dma("gpsimd", u_bf[e], sh_up if e == 0 else r_up[e - 1],
              CSEM["moe"])
        B.dma("gpsimd", d_bf[e], sh_down if e == 0 else r_down[e - 1],
              CSEM["moe"])
    CAST["moe"] = ("moe", CSEM["moe"].n)

    def mh_pass2():
        barrier()
        for i in (0, 1):
            if guard[i] is not None:
                B.wait("tensor", *guard[i])
                guard[i] = None
        s_mh = [B.sem("s_mh0"), B.sem("s_mh1")]
        sq_done = {}
        mm_done = {}
        it = 0
        for which, src, ntile, base in (
                ("k", k_d, NTH, 0), ("q", q_dram, 2, 64)):
            for h in range(NH):
                for t2 in range(ntile):
                    par = it % 2
                    tile = QKT[:, (2 + par) * 512:(3 + par) * 512]
                    sqt = SQT[:, par * TH:(par + 1) * TH]
                    if it >= 2:
                        B.wait("scalar", s_v, sq_done[it - 2])
                    lv = B.dma("scalar", tile,
                               src[h * 128:(h + 1) * 128,
                                   t2 * 512:(t2 + 1) * 512], s_mh[par])
                    B.wait("vector", s_mh[par], lv)
                    if it >= 2:
                        B.wait("vector", s_t, mm_done[it - 2])
                    sq_done[it] = B.emit(
                        "vector", lambda e, t=tile, q=sqt: e.tensor_mul(
                            q, t, t), inc=s_v)
                    B.wait("tensor", s_v, sq_done[it])
                    r = base + h + 16 * t2
                    mm_done[it] = B.emit(
                        "tensor", lambda e, q=sqt, r=r, st=(it == 0),
                        sp=(it == 95): e.matmul(
                            PA[0:96, 0:TH], SELB[:, 95 - r:191 - r], q,
                            start=st, stop=sp), inc=s_t)
                    it += 1
        B.wait("vector", s_t)
        mxc = SPV[:, 0:1]
        B.emit("vector", lambda e: e.reduce_max(
            mxc[0:96, :], PA[0:96, 0:TH], axis=AX.X), inc=s_v)
        B.wait("gpsimd", s_v)
        row = SPV[0:1, TH:TH + 96]
        B.dma("gpsimd", row, mxc[0:96, :], s_lg)
        B.wait("vector", s_lg)
        tmp = SPV[0:1, 2 * TH:2 * TH + 16]
        B.emit("vector", lambda e: e.tensor_max(
            MROW[0:1, 0:16], row[:, 0:16], row[:, 16:32]))
        B.emit("vector", lambda e: e.tensor_max(
            tmp, row[:, 32:48], row[:, 48:64]))
        B.emit("vector", lambda e: e.tensor_max(
            MROW[0:1, 16:32], row[:, 64:80], row[:, 80:96]), inc=s_v)
        B.wait("vector", s_v)
        B.emit("vector", lambda e: e.tensor_max(
            MROW[0:1, 0:16], MROW[0:1, 0:16], tmp), inc=s_v)
        B.wait("vector", s_v)
        B.emit("vector", lambda e: e.tensor_mul(
            MROW[0:1, 2 * NH:3 * NH], MROW[0:1, 0:NH], MROW[0:1, NH:2 * NH]),
            inc=s_v)
        B.wait("scalar", s_v)
        B.emit("scalar", lambda e: e.activation(
            MROW[0:1, 2 * NH:3 * NH], MROW[0:1, 2 * NH:3 * NH], AF.Sqrt),
            inc=s_s)
        B.wait("vector", s_s)
        B.emit("vector", lambda e: e.tensor_scalar_mul(
            MROW[0:1, 2 * NH:3 * NH], MROW[0:1, 2 * NH:3 * NH], -1.0001),
            inc=s_v)
        B.wait("tensor", s_v)
        B.emit("tensor", lambda e: e.matmul(
            PD[:, 0:NH], ONER, MROW[0:1, 2 * NH:3 * NH],
            start=True, stop=True), inc=s_t)
        B.wait("vector", s_t)
        B.emit("vector", lambda e: e.tensor_copy(
            CONST[:, C_MH:C_MH + NH], PD[:, 0:NH]), inc=s_v)
        B.wait("vector", s_v)
        B.emit("vector", lambda e: e.tensor_scalar(
            CONST[:, C_MHX:C_MHX + NH], CONST[:, C_MH:C_MH + NH],
            ccol(C_KTB), None, ALU.add), inc=s_v)
        B.emit("vector", lambda e: e.nop(), inc=s_v)
        guard[1] = (s_v, s_v.n)

    if phase == "B":
        barrier()
        B.wait("sync", s_st)
        for k in range(NCH):
            B.dma("sync", out_d[k * 128:(k + 1) * 128, :],
                  x1_d.bitcast(F32)[k * 128:(k + 1) * 128, TOFF:T], s_st)
        B.wait("sync", s_st)
        B.build()
        return nc

    mh_pass2()

    def attn_mix(qi):
        qtok = slice(qi * TH, (qi + 1) * TH)
        kt_max = (TOFF + (qi + 1) * TH) // 128
        s_q = [None, None]
        pv_guard = [None, None]
        exp_done = {}
        pv_done = {}
        barrier()
        if qi == 0:
            for j in range(4):
                B.dma("scalar", QKT[:, 3584 + j * 512:3584 + (j + 1) * 512],
                      mask_t[(8 + j) * 128:(9 + j) * 128, 0:512], s_ls)
            B.wait("vector", s_ls)

        for h in range(NH):
            par = h % 2
            qtile = QKT[:, 2560 + par * 512:2560 + (par + 1) * 512]
            if pv_guard[par] is not None:
                B.wait("sync", *pv_guard[par])
                B.wait("tensor", *pv_guard[par])
            qv = B.dma("sync", qtile, q_dram[h * 128:(h + 1) * 128, qtok],
                       s_qt2[par])
            B.wait("tensor", s_qt2[par], qv)
            pvp = PB[:, par * 512:par * 512 + 512]
            denp = PC[:, par * 512:par * 512 + 512]
            for kt in range(kt_max):
                kpar = kt % 2
                kb = kpar * 1280
                ktile = QKT[:, kb:kb + 128]
                vtile = QKT[:, kb + 128:kb + 256]
                etile = QKT[:, kb + 768:kb + 1280]
                key = (h, kt - 2)
                if kt < 2 and h > 0:
                    ktp = kt_max - 1 if (kt_max - 1) % 2 == kpar \
                        else kt_max - 2
                    key = (h - 1, ktp)
                if key in pv_done:
                    B.wait("sync", s_t, pv_done[key])
                B.dma("sync", ktile,
                      k_d[h * 128:(h + 1) * 128, kt * 128:(kt + 1) * 128],
                      s_kv2[kpar])
                lv = B.dma("sync", vtile,
                           v_d[h * T + kt * 128:h * T + (kt + 1) * 128, :],
                           s_kv2[kpar])
                sp_ = PA[:, kpar * 512:kpar * 512 + 512]
                if (h, kt - 2) in exp_done:
                    B.wait("tensor", s_s, exp_done[(h, kt - 2)])
                B.wait("tensor", s_kv2[kpar], lv)
                tv = B.emit("tensor", lambda e, p=sp_, w=ktile, r=qtile:
                            e.matmul(p, w, r, start=True, stop=True),
                            inc=s_t)
                B.wait("scalar", s_t, tv)
                bc = C_MHX + h if kt < 8 else C_MH + h
                ev = B.emit("scalar", lambda e, p=sp_, o=etile, bc=bc:
                            e.activation(o, p, AF.Exp, bias=ccol(bc)),
                            inc=s_s)
                exp_done[(h, kt)] = ev
                dj = kt - (8 + 4 * qi)
                if dj >= 0:
                    B.wait("vector", s_s, ev)
                    vv = B.emit("vector", lambda e, o=etile, j=dj:
                                e.tensor_tensor(
                                    o, o, QKT[:, 3584 + j * 512:
                                              3584 + (j + 1) * 512],
                                    ALU.mult), inc=s_v)
                    B.wait("tensor", s_v, vv)
                else:
                    B.wait("tensor", s_s, ev)
                B.emit("tensor", lambda e, p=pvp, w=vtile, r=etile,
                       st=(kt == 0), sp=(kt == kt_max - 1):
                       e.matmul(p, w, r, start=st, stop=sp))
                tv2 = B.emit("tensor", lambda e, p=denp, r=etile,
                             st=(kt == 0), sp=(kt == kt_max - 1):
                             e.matmul(p, ONEBF, r, start=st, stop=sp),
                             inc=s_t)
                pv_done[(h, kt)] = tv2
            B.wait("vector", s_t, pv_done[(h, kt_max - 1)])
            rt_ = SPV[:, (4 + par) * TH:(5 + par) * TH]
            rv = B.emit("vector", lambda e, d=denp, r=rt_: e.reciprocal(
                r, d), inc=s_v)
            B.wait("vector", s_v, rv)
            ov2 = B.emit("vector", lambda e, p=pvp, r=rt_,
                         d=SD[:, h * TH:(h + 1) * TH]: e.tensor_tensor(
                d, p, r, ALU.mult), inc=s_v)
            pv_guard[par] = (s_v, ov2)
        barrier(["tensor", "vector", "scalar", "sync"])

        for k in range(NCH):
            B.dma("sync", SB[:, k * TH:(k + 1) * TH],
                  xo_d[k * 128:(k + 1) * 128, qtok], s_x)
        XO_LD = s_x.n
        B.wait("vector", s_x, XO_LD)

        def evac_h1(m0, mw, p, tv):
            k = m0 // 128
            B.wait("vector", s_t, tv)
            v = B.emit("vector", lambda e, p=p, k=k: e.tensor_add(
                SB[:, k * TH:(k + 1) * TH], SB[:, k * TH:(k + 1) * TH], p),
                inc=s_v)
            return (s_v, v)

        gemm(ow_bf, NCH, D, lambda k: SD[:, k * TH:(k + 1) * TH], evac_h1,
             cast_ready=CAST["ow"])
        H1_OK = s_v.n
        B.wait("sync", s_v, H1_OK)
        for k in range(NCH):
            B.dma("sync", h1_d[k * 128:(k + 1) * 128, qtok],
                  SB[:, k * TH:(k + 1) * TH], s_st)
            if debug:
                B.dma("sync", dbg["h1"][k * 128:(k + 1) * 128, qtok],
                      SB[:, k * TH:(k + 1) * TH], s_st)

        B.wait("tensor", s_v, H1_OK)
        if guard[1] is not None:
            B.wait("tensor", *guard[1])
            guard[1] = None
        sq_r = PC[0:1, 512:512 + TH]
        sqv2 = {}
        for k in range(NCH):
            if k >= 1:
                B.wait("vector", s_t, sqv2[k - 1])
            B.emit("vector", lambda e, k=k: e.tensor_mul(
                SQT[:, 0:TH], SB[:, k * TH:(k + 1) * TH],
                SB[:, k * TH:(k + 1) * TH]), inc=s_v)
            B.wait("tensor", s_v)
            sqv2[k] = B.emit("tensor",
                             lambda e, st=(k == 0), sp=(k == NCH - 1):
                             e.matmul(sq_r, ONECR, SQT[:, 0:TH],
                                      start=st, stop=sp), inc=s_t)
        B.wait("vector", s_t)
        r = SPV[0:1, (8 + qi) * TH:(9 + qi) * TH]
        B.emit("vector", lambda e: e.tensor_scalar_mul(r, sq_r, 1.0 / D),
               inc=s_v)
        B.wait("scalar", s_v)
        B.emit("scalar", lambda e: e.activation(r, r, AF.Ln,
                                                bias=ccol(C_EPS)[0:1, :]))
        B.emit("scalar", lambda e: e.activation(r, r, AF.Exp, scale=-0.5),
               inc=s_s)
        B.wait("vector", s_s)
        B.emit("vector", lambda e: e.nop(), inc=s_v)
        guard[1] = (s_v, s_v.n)
        if guard[1] is not None:
            B.wait("tensor", *guard[1])
            guard[1] = None
        B.emit("tensor", lambda e: e.matmul(
            PD[:, 0:TH], ONER, SPV[0:1, (8 + qi) * TH:(9 + qi) * TH],
            start=True, stop=True), inc=s_t)
        B.wait("vector", s_t)
        for k in range(NCH):
            B.emit("vector", lambda e, k=k: e.scalar_tensor_tensor(
                SF[:, k * 1024 + qi * TH:k * 1024 + (qi + 1) * TH],
                SB[:, k * TH:(k + 1) * TH], ccol(C_LN2 + k), PD[:, 0:TH],
                ALU.mult, ALU.mult), inc=s_v)
        RMS2_OK = s_v.n
        B.emit("vector", lambda e: e.nop(), inc=s_v)
        guard[1] = (s_v, s_v.n)

        for tc in range(4):
            if guard[0] is not None:
                B.wait("tensor", *guard[0])
                guard[0] = None
            lgmm = {}
            for k in range(NCH):
                w2 = WL2[:, (k % 2) * NR:(k % 2 + 1) * NR]
                if k >= 2:
                    B.wait("vector", s_t, lgmm[k - 2])
                wlv = B.emit("vector", lambda e, k=k, w2=w2: e.tensor_scalar(
                    w2, RW[:, k * NR:(k + 1) * NR], ccol(C_LN2 + k), None,
                    ALU.mult), inc=s_v)
                B.wait("tensor", s_v, wlv)
                lgmm[k] = B.emit("tensor", lambda e, k=k, tc=tc, w2=w2,
                                 st=(k == 0), sp=(k == NCH - 1): e.matmul(
                    PA[:, 0:NR],
                    SB[:, k * TH + tc * 128:k * TH + (tc + 1) * 128],
                    w2, start=st, stop=sp), inc=s_t)
            LG_MM = s_t.n
            if guard[1] is not None:
                B.wait("tensor", *guard[1])
                guard[1] = None
            B.emit("tensor", lambda e, tc=tc: e.matmul(
                PD[:, 1:2],
                SPV[0:1, (8 + qi) * TH + tc * 128:(8 + qi) * TH + tc * 128 + 128],
                ONE1, start=True, stop=True), inc=s_t)
            B.wait("vector", s_t)
            lgt = SPV[:, 4 * TH:4 * TH + NR]
            B.emit("vector", lambda e: e.tensor_copy(
                SPV[:, 5 * TH:5 * TH + 1], PD[:, 1:2]), inc=s_v)
            B.wait("vector", s_v)
            guard[1] = (s_v, s_v.n)
            B.emit("vector", lambda e: e.scalar_tensor_tensor(
                lgt, PA[:, 0:NR], SPV[:, 5 * TH:5 * TH + 1], RB,
                ALU.mult, ALU.add), inc=s_v)
            B.wait("vector", s_v)
            guard[0] = (s_v, s_v.n)
            m1 = SPV[:, 5 * TH + 4:5 * TH + 5]
            m2 = SPV[:, 5 * TH + 5:5 * TH + 6]
            msk = SPV[:, 5 * TH + 8:5 * TH + 8 + NR]
            prb = SPV[:, 6 * TH:6 * TH + NR]
            B.emit("vector", lambda e: e.reduce_max(m1, lgt, axis=AX.X),
                   inc=s_v)
            B.wait("vector", s_v)
            B.emit("vector", lambda e: e.tensor_scalar(
                msk, lgt, m1, None, ALU.is_ge), inc=s_v)
            B.wait("vector", s_v)
            B.emit("vector", lambda e: e.scalar_tensor_tensor(
                msk, msk, -1e30, lgt, ALU.mult, ALU.add), inc=s_v)
            B.wait("vector", s_v)
            B.emit("vector", lambda e: e.reduce_max(m2, msk, axis=AX.X),
                   inc=s_v)
            B.wait("vector", s_v)
            B.emit("vector", lambda e: e.tensor_scalar(
                msk, lgt, m2, None, ALU.is_ge), inc=s_v)
            B.wait("scalar", s_v)
            B.emit("scalar", lambda e: e.activation(prb, lgt, AF.Sigmoid),
                   inc=s_s)
            B.wait("vector", s_s)
            B.emit("vector", lambda e: e.tensor_mul(msk, msk, prb), inc=s_v)
            B.wait("vector", s_v)
            B.emit("vector", lambda e: e.tensor_reduce(
                m1, msk, axis=AX.X, op=ALU.add), inc=s_v)
            B.wait("vector", s_v)
            B.emit("vector", lambda e: e.reciprocal(m2, m1), inc=s_v)
            B.wait("vector", s_v)
            B.emit("vector", lambda e, qi=qi, tc=tc: e.tensor_scalar(
                WTOK[:, (qi * 4 + tc) * NR:(qi * 4 + tc + 1) * NR],
                msk, m2, None, ALU.mult), inc=s_v)
            B.wait("vector", s_v)
        if debug:
            B.wait("sync", s_v)
            B.dma("sync", dbg["w"], WTOK, s_st)

    for qi in range(2):
        attn_mix(qi)

    if phase == "BC":
        barrier()
        for k in range(NCH):
            B.dma("sync", out_d[k * 128:(k + 1) * 128, :],
                  h1_d[k * 128:(k + 1) * 128, :], s_st)
        B.wait("sync", s_st)
        B.build()
        return nc

    barrier()
    B.wait("sync", CSEM[CAST["moe"][0]], CAST["moe"][1])
    OUT = {0: SA, 1: SB}
    for k in range(NCH):
        B.dma("sync", SA[:, k * TH:(k + 1) * TH],
              x1_d[k * 128:(k + 1) * 128, TOFF:TOFF + TH], s_x)
    B.wait("vector", s_x)
    for k in range(NCH):
        lv = B.dma("scalar", SPV[:, 4 * TH:5 * TH],
                   h1_d[k * 128:(k + 1) * 128, 0:TH], s_ls)
        B.wait("vector", s_ls, lv)
        v = B.emit("vector", lambda e, k=k: e.tensor_add(
            SA[:, k * TH:(k + 1) * TH], SA[:, k * TH:(k + 1) * TH],
            SPV[:, 4 * TH:5 * TH]), inc=s_v)
        B.wait("scalar", s_v, v)
    for k in range(NCH):
        lv = B.dma("scalar", SPV[:, 4 * TH:5 * TH],
                   x1_d.bitcast(F32)[k * 128:(k + 1) * 128,
                                     TOFF + TH:TOFF + 2 * TH], s_ls)
        B.wait("vector", s_ls, lv)
        v = B.emit("vector", lambda e, k=k: e.tensor_add(
            SB[:, k * TH:(k + 1) * TH], SB[:, k * TH:(k + 1) * TH],
            SPV[:, 4 * TH:5 * TH]), inc=s_v)
        B.wait("scalar", s_v, v)
    barrier()

    for ei in range(8):
        routed = ei > 0
        er = ei - 1
        wg_d, wu_d, wd_d = g_bf[ei], u_bf[ei], d_bf[ei]
        if routed:
            B.wait("gpsimd", s_v)
            if guard[1] is not None:
                B.wait("tensor", *guard[1])
                guard[1] = None
            for tcq in range(8):
                B.dma("gpsimd", SPV[0:1, TOFF + tcq * 128:
                                    TOFF + (tcq + 1) * 128],
                      WTOK[:, tcq * NR + er:tcq * NR + er + 1], s_lg)
            B.wait("vector", s_lg)
            B.emit("vector", lambda e: e.tensor_scalar_mul(
                SPV[0:1, 0:TOFF], SPV[0:1, TOFF:2 * TOFF], 0.125), inc=s_v)
            B.wait("tensor", s_v)
            for qi in range(2):
                B.emit("tensor", lambda e, qi=qi: e.matmul(
                    PD[:, qi * 512:(qi + 1) * 512], ONER,
                    SPV[0:1, qi * 512:(qi + 1) * 512], start=True,
                    stop=True), inc=(s_t if qi == 1 else None))
            B.wait("vector", s_t)
            for qi in range(2):
                B.emit("vector", lambda e, qi=qi: e.tensor_copy(
                    SPV[:, (6 + qi) * TH:(7 + qi) * TH],
                    PD[:, qi * 512:(qi + 1) * 512]), inc=s_v)
            B.wait("tensor", s_v)
            B.emit("vector", lambda e: e.nop(), inc=s_v)
            guard[1] = (s_v, s_v.n)
        WB_OK = s_v.n

        for qi in range(2):
            wbt = SPV[:, (6 + qi) * TH:(7 + qi) * TH]
            for ib in range(2):
                for wsel, wdram, pb in ((0, wg_d, (PA, PB)),
                                        (1, wu_d, (PC, PD))):
                    if guard[wsel] is not None:
                        B.wait("tensor", *guard[wsel])
                        guard[wsel] = None
                    for k in range(NCH):
                        si = k % 4
                        if bslab_guard[si] is not None:
                            B.wait("sync", *bslab_guard[si])
                        wv = B.dma("sync",
                                   WBUFB[:, si * 512:si * 512 + 512],
                                   wdram[k * 128:(k + 1) * 128,
                                         ib * 512:(ib + 1) * 512], s_sl[si])
                        B.wait("tensor", s_sl[si], wv)
                        tv = None
                        for mc in range(4):
                            pt = pb[mc // 2][:, (mc % 2) * 512:
                                             (mc % 2) * 512 + 512]
                            tv = B.emit(
                                "tensor",
                                lambda e, p=pt,
                                w=WBUFB[:, si * 512 + mc * 128:
                                        si * 512 + (mc + 1) * 128],
                                r=SF[:, k * 1024 + qi * TH:
                                     k * 1024 + (qi + 1) * TH],
                                st=(k == 0), sp=(k == NCH - 1):
                                e.matmul(p, w, r, start=st, stop=sp),
                                inc=(s_t if mc == 3 else None))
                        bslab_guard[si] = (s_t, tv)
                    guard[wsel] = (s_t, s_t.n)
                GU = s_t.n
                last = None
                for mc in range(4):
                    gp = (PA, PB)[mc // 2][:, (mc % 2) * 512:
                                           (mc % 2) * 512 + 512]
                    up = (PC, PD)[mc // 2][:, (mc % 2) * 512:
                                           (mc % 2) * 512 + 512]
                    sg = SPV[:, 4 * TH:5 * TH]
                    B.wait("scalar", s_t, GU)
                    B.wait("scalar", s_v)
                    sv = B.emit("scalar", lambda e, g=gp, s=sg: e.activation(
                        s, g, AF.Silu), inc=s_s)
                    B.wait("vector", s_s, sv)
                    hv = B.emit("vector", lambda e, s=sg, u=up: e.tensor_tensor(
                        SPV[:, 5 * TH:6 * TH], s, u, ALU.mult), inc=s_v)
                    B.wait("vector", s_v, hv)
                    ic = ib * 4 + mc
                    if routed:
                        hv2 = B.emit("vector", lambda e, ic=ic, qi=qi, w=wbt:
                                     e.tensor_tensor(
                                         SE[:, ic * 1024 + qi * TH:
                                            ic * 1024 + (qi + 1) * TH],
                                         SPV[:, 5 * TH:6 * TH], w, ALU.mult),
                                     inc=s_v)
                    else:
                        hv2 = B.emit("vector", lambda e, ic=ic, qi=qi:
                                     e.tensor_scalar_mul(
                                         SE[:, ic * 1024 + qi * TH:
                                            ic * 1024 + (qi + 1) * TH],
                                         SPV[:, 5 * TH:6 * TH], 0.125),
                                     inc=s_v)
                    last = (s_v, hv2)
                guard[0] = last
                guard[1] = last
            HID_OK = s_v.n

            for mb in range(4):
                pbank = (PA, PB) if mb % 2 == 0 else (PC, PD)
                if guard[mb % 2] is not None:
                    B.wait("tensor", *guard[mb % 2])
                    guard[mb % 2] = None
                for k in range(ICH):
                    si = k % 4
                    if bslab_guard[si] is not None:
                        B.wait("sync", *bslab_guard[si])
                    wv = B.dma("sync", WBUFB[:, si * 512:si * 512 + 512],
                               wd_d[k * 128:(k + 1) * 128,
                                    mb * 512:(mb + 1) * 512], s_sl[si])
                    B.wait("tensor", s_sl[si], wv)
                    if k == 0 and mb == 0:
                        B.wait("tensor", s_v, HID_OK)
                    tv = None
                    for mc in range(4):
                        pt = pbank[mc // 2][:, (mc % 2) * 512:
                                            (mc % 2) * 512 + 512]
                        tv = B.emit(
                            "tensor",
                            lambda e, p=pt,
                            w=WBUFB[:, si * 512 + mc * 128:
                                    si * 512 + (mc + 1) * 128],
                            r=SE[:, k * 1024 + qi * TH:
                                 k * 1024 + (qi + 1) * TH],
                            st=(k == 0), sp=(k == ICH - 1):
                            e.matmul(p, w, r, start=st, stop=sp),
                            inc=(s_t if mc == 3 else None))
                    bslab_guard[si] = (s_t, tv)
                DN = s_t.n
                last = None
                for mc in range(4):
                    pt = pbank[mc // 2][:, (mc % 2) * 512:(mc % 2) * 512 + 512]
                    kk = mb * 4 + mc
                    B.wait("vector", s_t, DN)
                    last = B.emit("vector", lambda e, p=pt, kk=kk, qi=qi:
                                  e.tensor_add(OUT[qi][:, kk * TH:(kk + 1) * TH],
                                               OUT[qi][:, kk * TH:(kk + 1) * TH],
                                               p), inc=s_v)
                    last = (s_v, last)
                guard[mb % 2] = last

    barrier()
    for k in range(NCH):
        B.dma("sync", out_d[k * 128:(k + 1) * 128, 0:TH].bitcast(F32R),
              SA[:, k * TH:(k + 1) * TH], s_st)
        B.dma("sync", out_d[k * 128:(k + 1) * 128, TH:2 * TH],
              SB[:, k * TH:(k + 1) * TH], s_st)
    B.wait("sync", s_st)
    barrier()
    B.build()
    return nc


def make_inputs(inp):
    x = np.ascontiguousarray(inp["L0_attn_x"], dtype=np.float32)
    wout = np.asarray(inp["L0_attn_W_out"], dtype=np.float32)
    perm = []
    for g in range(G):
        perm.extend(range(g * 128, (g + 1) * 128))
        perm.extend(range((8 + g) * 128, (9 + g) * 128))
    wout_p = np.ascontiguousarray(wout[np.asarray(perm)])
    shared = {
        "W_in": inp["L0_attn_W_in"],
        "Wg_x": inp["L0_attn_Wg_x"],
        "Wg_a": inp["L0_attn_Wg_a"],
        "W_out": wout_p,
        "b_in": inp["L0_attn_b_in"],
        "bg_x": inp["L0_attn_bg_x"],
        "bg_a": inp["L0_attn_bg_a"],
        "b_out": inp["L0_attn_b_out"],
        "a_real": inp["L0_attn_a_real_param"],
        "a_imag": inp["L0_attn_a_imag_param"],
        "kv_d_w": inp["L0_ffn_kv_d_w"],
        "q_d_w": inp["L0_ffn_q_d_w"],
        "k_u_w": inp["L0_ffn_k_u_w"],
        "v_u_w": inp["L0_ffn_v_u_w"],
        "q_u_w": inp["L0_ffn_q_u_w"],
        "rope_k_w": inp["L0_ffn_rope_k_w"],
        "rope_q_w": inp["L0_ffn_rope_q_w"],
        "o_w": inp["L0_ffn_o_w"],
        "ln1_w": inp["L0_ffn_ln1_w"],
        "ln2_w": inp["L0_ffn_ln2_w"],
        "sh_gate": inp["L0_ffn_sh_gate"],
        "sh_up": inp["L0_ffn_sh_up"],
        "sh_down": inp["L0_ffn_sh_down"],
        "r_gate": inp["L0_ffn_r_gate"],
        "r_up": inp["L0_ffn_r_up"],
        "r_down": inp["L0_ffn_r_down"],
        "router_w": inp["L0_ffn_router_w"],
        "router_b": inp["L0_ffn_router_b"],
        "ones_colr": np.ones((128, 1), np.float32),
        "sel_buf": np.concatenate([np.zeros((128, 95), np.float32),
                                   np.ones((128, 1), np.float32),
                                   np.zeros((128, 95), np.float32)], 1),
        "ones_row": np.ones((1, 128), np.float32),
        "one_1": np.ones((1, 1), np.float32),
        "ones_bf": np.ones((128, 128), BF),
    }
    shared = {k: np.ascontiguousarray(v, dtype=(
        BF if k == "ones_bf" else np.float32)) for k, v in shared.items()}

    inv_freq = 1.0 / 10000.0 ** (np.arange(0, RD, 2, dtype=np.float64) / RD)

    def rope_tabs(pos, scale):
        ang = np.outer(pos.astype(np.float64), inv_freq)
        emb = np.concatenate([ang, ang], axis=-1)
        cos = np.cos(emb).T.astype(np.float32)
        sin = np.sin(emb).T.astype(np.float32)
        sinS = sin.copy()
        sinS[:32] *= -1.0
        c2 = np.ascontiguousarray(np.tile(cos * scale, (2, 1)))
        s2 = np.ascontiguousarray(np.tile(sinS * scale, (2, 1)))
        return c2, s2

    in_maps = []
    kidx = np.arange(T)[:, None]
    for c in range(8):
        b, hf = c // 2, c % 2
        m = dict(shared)
        if hf == 0:
            xp = np.concatenate(
                [np.zeros((TOFF, D), np.float32), x[b, 0:TOFF]], 0)
            pos = np.concatenate(
                [np.zeros(TOFF), np.arange(TOFF)]).astype(np.float64)
            gaterow = np.concatenate(
                [np.zeros((1, TOFF)), np.ones((1, TOFF))], 1)
            qpad = (TOFF + np.arange(TOFF))[None, :]
            maskv = ((kidx >= TOFF) & (kidx <= qpad))
        else:
            xp = x[b]
            pos = np.arange(T).astype(np.float64)
            gaterow = np.ones((1, T))
            qpad = (TOFF + np.arange(TOFF))[None, :]
            maskv = (kidx <= qpad)
        m["xT"] = np.ascontiguousarray(xp.T)
        m["gate_row"] = np.ascontiguousarray(gaterow, np.float32)
        m["ktb_col"] = np.full((128, 1), -1e30 if hf == 0 else 0.0,
                               np.float32)
        m["mask_t"] = np.ascontiguousarray(maskv.astype(BF))
        ck, sk = rope_tabs(pos, 1.0)
        cq, sq = rope_tabs(pos, QSCALE)
        m["cosk_t"], m["sinkS_t"] = ck, sk
        m["cosq_t"], m["sinqS_t"] = cq, sq
        in_maps.append(m)
    return in_maps


def kernel(**inputs):
    inp = {k: np.asarray(v) for k, v in inputs.items()}
    nc = build_program()
    in_maps = make_inputs(inp)
    res = run_bass_kernel_spmd(nc, in_maps, core_ids=list(range(8)))
    y = np.empty((B_, S_, D), np.float32)
    for c in range(8):
        b, hf = c // 2, c % 2
        y[b, hf * TOFF:(hf + 1) * TOFF, :] = \
            np.asarray(res.results[c]["out"]).T
    return y
